# version 02
# baseline (speedup 1.0000x reference)
import math
import numpy as np
import concourse.bass as bass
import concourse.mybir as mybir
import concourse.bass_utils as _bass_utils

F32 = mybir.dt.float32
BF16 = mybir.dt.bfloat16
AF = mybir.ActivationFunctionType
ALU = mybir.AluOpType
AX = mybir.AxisListType.X

NCORES = 8
D = 1024
NCT = D // 128


ENGS = ("tensor", "vector", "scalar", "gpsimd", "sync")


class Op:
    __slots__ = ("eng", "fn", "r", "w", "dma", "idx", "deps", "need_inc", "pub", "tag", "ring", "top")

    def __init__(self, eng, fn, r, w, dma, idx, tag):
        self.eng, self.fn, self.r, self.w = eng, fn, tuple(r), tuple(w)
        self.dma, self.idx, self.tag = dma, idx, tag
        self.deps, self.need_inc, self.pub = [], False, None
        self.ring = None
        self.top = False


class K:
    def __init__(self, nc, T=2048):
        self.nc = nc
        self.T = T
        self.NTT = T // 128
        self.NCH = T // 512
        self.ops = []
        self.const_cols = {}
        self.const_buf = None
        self._uid = 0
        self.fence_pos = []
        self.arena = None
        self.arena_base = 0
        self.arena_off = 0
        self.arena_size = 0

    def uid(self):
        self._uid += 1
        return self._uid

    def sb(self, name, shape, dtype, persist=False):
        if persist or self.arena is None:
            return self.nc.alloc_sbuf_tensor(f"{name}_{self.uid()}", shape, dtype)
        nbytes = int(np.prod(shape[1:])) * mybir.dt.size(dtype)
        off = (self.arena_off + 31) & ~31
        assert off + nbytes <= self.arena_size, \
            f"arena overflow at {name}: {off}+{nbytes} > {self.arena_size}"
        self.arena_off = off + nbytes
        return self.nc.alloc_sbuf_tensor_at(f"{name}_{self.uid()}", shape, dtype,
                                            offset=self.arena_base + off)

    def arena_setup(self, reserve=512):
        rem = self.nc.sbuf_bytes_remaining - reserve
        self.arena = self.nc.alloc_sbuf_tensor("ARENA", [128, rem], mybir.dt.uint8)
        from concourse.bass_types import SBTensorHandle
        mloc = self.nc.lookup_mloc(self.arena)
        self.arena_base = mloc.addr
        self.arena_size = rem
        self.arena_off = 0

    def arena_reset(self):
        self.fence_pos.append(len(self.ops))
        self.arena_off = 0

    def barrier(self):
        self.fence_pos.append(len(self.ops))

    def arena_mark(self):
        return self.arena_off

    def arena_release(self, mark):
        self.fence_pos.append(len(self.ops))
        self.arena_off = mark

    def dram(self, name, shape, dtype=F32, kind="Internal", addr_space="Local"):
        return self.nc.dram_tensor(name, shape, dtype, kind=kind,
                                   addr_space=addr_space)

    def psum_setup(self):
        self.PS = {}
        for nm in ("psA0", "psA1", "psB0", "psB1", "psC0", "psC1"):
            self.PS[nm] = self.nc.alloc_psum_tensor(nm, [128, 512], F32)
        for nm in ("psT0", "psT1"):
            self.PS[nm] = self.nc.alloc_psum_tensor(nm, [128, 1024], BF16)

    def cc(self, value):
        value = float(value)
        if value not in self.const_cols:
            self.const_cols[value] = len(self.const_cols)
            assert len(self.const_cols) <= 32
        return ("const", value)

    def _rc(self, v, pn=128):
        if isinstance(v, tuple) and len(v) == 2 and v[0] == "const":
            c = self.const_cols[v[1]]
            return self.const_buf[0:pn, c:c + 1]
        return v

    def emit(self, eng, fn, r=(), w=(), dma=False, tag="", top=False):
        op = Op(eng, fn, r, w, dma, len(self.ops), tag)
        op.top = top
        self.ops.append(op)
        return op

    def dmac(self, dst, src, r=(), w=(), q="sync", tag=""):
        return self.emit(q, lambda e: e.dma_start(dst, src), r=r, w=w, dma=True, tag=tag)

    def dmacast(self, dst, src, r=(), w=(), tag=""):
        return self.dmac(dst, src, r=r, w=w, q="gpsimd", tag=tag)

    def mm(self, out, lhsT, rhs, start, stop, r=(), w=(), tag="", tile_position=None):
        return self.emit("tensor",
                         lambda e: e.matmul(out, lhsT, rhs, start=start, stop=stop,
                                            tile_position=tile_position),
                         r=r, w=w, tag=tag)

    def trans(self, out, in_, ident_ap, r=(), w=(), tag=""):
        return self.emit("tensor", lambda e: e.transpose(out, in_, ident_ap),
                         r=tuple(r) + ("c.ident",), w=w, tag=tag)

    def act(self, out, in_, func, bias=0.0, scale=1.0, accum=None, r=(), w=(), tag=""):
        if isinstance(bias, float) and bias != 0.0:
            raise ValueError("float bias needs k.cc(): " + tag)
        if isinstance(scale, float) and scale != 1.0:
            raise ValueError("float scale needs k.cc(): " + tag)
        if not isinstance(bias, float) or not isinstance(scale, float):
            r = tuple(r) + ("c.consts",)
        pn = out.ap[0][1]
        return self.emit("scalar",
                         lambda e: e.activation(out, in_, func, bias=self._rc(bias, pn),
                                                scale=self._rc(scale, pn),
                                                accum_out=accum), r=r, w=w, tag=tag)

    def v(self, meth, *args, r=(), w=(), eng="vector", tag="", **kw):
        if any((isinstance(a, tuple) and len(a) == 2 and a[0] == "const") for a in args):
            r = tuple(r) + ("c.consts",)

        lowp = kw.pop("lowp", False)

        def fn(e, meth=meth, args=args, kw=kw, lowp=lowp):
            if lowp:
                with self.nc.allow_low_precision("bf16 path"):
                    return getattr(e, meth)(*(self._rc(a) for a in args), **kw)
            return getattr(e, meth)(*(self._rc(a) for a in args), **kw)
        return self.emit(eng, fn, r=r, w=w, tag=tag)

    def setup_common(self):
        self.psum_setup()
        self.ident_bf = self.sb("c_ident", [128, 128], BF16)
        cd = self.dram("c_ident_f32", [128, 128], F32, kind="ExternalInput")
        self.dmacast(self.ident_bf[:], cd.ap(), w=("c.ident",), tag="ld ident")
        self.ones_col = self.sb("c_ones_col", [128, 1], BF16)
        self.emit("gpsimd", lambda e: e.memset(self.ones_col[:], 1.0),
                  w=("c.ones_col",), tag="ones col")
        self.ones_row = self.sb("c_ones_row", [1, 128], BF16)
        self.emit("gpsimd", lambda e: e.memset(self.ones_row[:], 1.0),
                  w=("c.ones_row",), tag="ones row")
        self.const_buf = self.nc.alloc_sbuf_tensor("c_consts", [128, 32], F32)
        self.arena_setup()

    def ident(self, base, n=None):
        n = n or 128 - base
        return self.ident_bf[base:base + n, base:base + n]

    def _finish_consts(self):
        if self.const_cols:
            front = []
            for val, col in sorted(self.const_cols.items(), key=lambda kv: kv[1]):
                front.append(Op("gpsimd",
                                (lambda e, b=self.const_buf, c=col, v=val:
                                 e.memset(b[:, c:c + 1], v)),
                                (), ("c.consts",), False, -1, "const"))
            self.ops = front + self.ops

    def build(self):
        nfront = len(self.ops)
        self._finish_consts()
        shift = len(self.ops) - nfront
        fence_pos = [p + shift for p in self.fence_pos]
        for i, op in enumerate(self.ops):
            op.idx = i
        RING = 10
        dma_ring, dma_seq = {}, {}
        for op in self.ops:
            if op.dma:
                op.need_inc = True
                n = dma_seq.get(op.eng, 0)
                dma_seq[op.eng] = n + 1
                op.ring = f"{op.eng}dma{n % RING}"
                lst = dma_ring.setdefault(op.eng, [])
                if n >= RING:
                    op.deps = [lst[n - RING]]
                lst.append(op.idx)

        def semname(o):
            return o.ring if o.dma else o.eng

        last_w, readers = {}, {}
        last_per_sem = {}
        fence_deps = []
        fi = 0
        for op in self.ops:
            while fi < len(fence_pos) and op.idx >= fence_pos[fi]:
                fence_deps = sorted(last_per_sem.values())
                fi += 1
            deps = set(op.deps) | set(fence_deps)
            for kk in op.r:
                j = last_w.get(kk)
                if j is not None:
                    deps.add(j)
            for kk in op.w:
                j = last_w.get(kk)
                if j is not None:
                    deps.add(j)
                deps.update(readers.get(kk, ()))
            deps.discard(op.idx)
            deps = [j for j in sorted(deps)
                    if not (op.eng == "tensor" and not op.dma
                            and self.ops[j].eng == "tensor" and not self.ops[j].dma)]
            op.deps = deps
            for j in deps:
                self.ops[j].need_inc = True
            for kk in op.r:
                readers.setdefault(kk, []).append(op.idx)
            for kk in op.w:
                last_w[kk] = op.idx
                readers[kk] = []
            last_per_sem[semname(op)] = op.idx

        cum = {}
        for op in self.ops:
            s = semname(op)
            c = cum.get(s, 0)
            if op.need_inc:
                c += 16 if op.dma else 1
            cum[s] = c
            op.pub = c
        nc = self.nc
        sems = {s: nc.alloc_semaphore(name="sem_" + s) for s in cum}
        self._sems = sems
        allops = self.ops
        state = {e: {} for e in ENGS}

        def emit_one(e, op, satisfied):
            for j in op.deps:
                pj = allops[j]
                sname = semname(pj)
                need = pj.pub
                assert pj.need_inc
                if satisfied.get(sname, -1) >= need:
                    continue
                e.wait_ge(sems[sname], need)
                satisfied[sname] = need
            self._active_sem = (sems[semname(op)], 16 if op.dma else 1)
            inst = op.fn(e)
            if op.need_inc and inst != "incs_attached":
                s = semname(op)
                assert inst is not None, f"op {op.tag}: no inst to inc"
                inst.then_inc(sems[s], 16 if op.dma else 1)

        segments = []
        cur = []
        for op in self.ops:
            if op.top:
                segments.append(("blk", cur))
                segments.append(("top", [op]))
                cur = []
            else:
                cur.append(op)
        segments.append(("blk", cur))
        for kind_, seg in segments:
            if not seg:
                continue
            if kind_ == "top":
                for op in seg:
                    emit_one(getattr(nc, op.eng), op, state[op.eng])
            else:
                per_eng = {e: [o for o in seg if o.eng == e] for e in ENGS}

                def run_engine(e, eng_name):
                    for op in per_eng[eng_name]:
                        emit_one(e, op, state[eng_name])
                with nc.Block() as blk:
                    blk.tensor(lambda e: run_engine(e, "tensor"))
                    blk.vector(lambda e: run_engine(e, "vector"))
                    blk.scalar(lambda e: run_engine(e, "scalar"))
                    blk.gpsimd(lambda e: run_engine(e, "gpsimd"))
                    blk.sync(lambda e: run_engine(e, "sync"))
        return sems



def host_tiles(w, mw=128, cw=128):
    w = np.ascontiguousarray(np.asarray(w, np.float32))
    C, M = w.shape
    n_c = (C + cw - 1) // cw
    n_m = (M + mw - 1) // mw
    widths = [min(mw, M - i * mw) for i in range(n_m)]
    parts = []
    for mt in range(n_m):
        wm = widths[mt]
        slab = np.zeros((cw, n_c * wm), np.float32)
        for ct in range(n_c):
            cl = min(cw, C - ct * cw)
            slab[:cl, ct * wm:(ct + 1) * wm] = w[ct * cw:ct * cw + cl,
                                                 mt * mw:mt * mw + wm]
        parts.append(slab.reshape(-1))
    return np.concatenate(parts), (n_m, n_c, widths, C)


class WStream:

    def __init__(self, k, name, dram_t, meta, nbuf=2):
        self.k, self.name, self.dram = k, name, dram_t
        self.n_m, self.n_c, self.widths, self.C = meta
        self.nbuf = nbuf
        self.wmax = max(self.widths)
        self.slab = k.sb("ws_" + name, [128, nbuf * self.n_c * self.wmax], BF16)
        self.offs = np.concatenate([[0], np.cumsum(
            [self.n_c * 128 * w for w in self.widths])]).astype(np.int64)

    def load(self, mt, b=None):
        k = self.k
        b = (mt % self.nbuf) if b is None else b
        w = self.widths[mt]
        src = self.dram.ap()[int(self.offs[mt]):int(self.offs[mt + 1])] \
            .rearrange("(p cf) -> p cf", p=128)
        base = b * self.n_c * self.wmax
        dst = self.slab[:, base:base + self.n_c * w]
        k.dmacast(dst, src, w=(f"ws.{self.name}.{b}",), tag=f"ws {self.name} mt{mt}")

    def tile(self, mt, ct, b=None):
        b = (mt % self.nbuf) if b is None else b
        w = self.widths[mt]
        base = b * self.n_c * self.wmax + ct * w
        return self.slab[:, base:base + w], f"ws.{self.name}.{b}"


def vec_pack(*vs):
    offs, parts, o = [], [], 0
    for v in vs:
        v = np.asarray(v, np.float32).reshape(-1)
        offs.append(o)
        parts.append(v)
        o += len(v)
        pad = (-o) % 128
        if pad:
            parts.append(np.zeros(pad, np.float32))
            o += pad
    return np.concatenate(parts) if parts else np.zeros(1, np.float32), offs



def emit_norm(k, x_dram, xln, kind, gain_sb=None, gkey=None, eps=1e-5, pfx="ln"):
    T, PS = k.T, k.PS
    _mk = k.arena_mark()
    xf = k.sb(pfx + "_xf", [128, T], F32)
    sq = k.sb(pfx + "_sq", [128, 1024], BF16)
    rows = k.sb(pfx + "_rows", [1, 1024 * 2], F32)
    rows_bf = k.sb(pfx + "_rowsb", [1, 2048], BF16)
    mbc = k.sb(pfx + "_mbc", [128, 1024], BF16)
    rbc = k.sb(pfx + "_rbc", [128, 1024], BF16)

    if isinstance(x_dram, tuple) and x_dram[0] == "sb":
        srcbuf, keyfn = x_dram[1], x_dram[2]
        for ct in range(NCT):
            k.v("tensor_copy", xln[:, ct * T:(ct + 1) * T],
                srcbuf[:, ct * T:(ct + 1) * T], r=tuple(keyfn(ct)),
                w=(f"{pfx}.xb{ct}",), tag=f"{pfx} cast{ct}")
    else:
        for ct in range(NCT):
            kxf = f"{pfx}.xf"
            k.dmac(xf[:], x_dram.ap()[ct * 128:(ct + 1) * 128, :], w=(kxf,),
                   tag=f"{pfx} ldx{ct}")
            k.v("tensor_copy", xln[:, ct * T:(ct + 1) * T], xf[:], r=(kxf,),
                w=(f"{pfx}.xb{ct}",), tag=f"{pfx} cast{ct}")

    nhalf = max(1, T // 1024)
    hw = min(T, 1024)
    for half in range(nhalf):
        hs = slice(half * hw, (half + 1) * hw)
        for ct in range(NCT):
            xlv = xln[:, ct * T + half * hw: ct * T + (half + 1) * hw]
            k.v("tensor_mul", sq[:, 0:hw], xlv, xlv, r=(f"{pfx}.xb{ct}",),
                w=(f"{pfx}.sq",), tag=f"{pfx} sq{half}.{ct}")
            for ch in range(hw // 512):
                sl = slice(ch * 512, (ch + 1) * 512)
                if kind == "ln":
                    k.mm(PS["psA" + str(ch)][0:1, :], k.ones_col[:], xlv[:, sl],
                         start=(ct == 0), stop=(ct == NCT - 1),
                         r=(f"{pfx}.xb{ct}", "c.ones_col"), w=(f"psA{ch}",),
                         tag=f"{pfx} mmS{half}.{ct}.{ch}")
                k.mm(PS["psB" + str(ch)][0:1, :], k.ones_col[:], sq[:, sl],
                     start=(ct == 0), stop=(ct == NCT - 1),
                     r=(f"{pfx}.sq", "c.ones_col"), w=(f"psB{ch}",),
                     tag=f"{pfx} mmQ{half}.{ct}.{ch}")
        mean, scr = rows[0:1, 0:1024], rows[0:1, 1024:2048]
        rstd = scr
        rA = ("psA0", "psA1")[:hw // 512]
        rB = ("psB0", "psB1")[:hw // 512]
        psA2 = PS["psA0"].tensor if False else None
        if kind == "ln":
            for ch in range(hw // 512):
                k.v("tensor_scalar_mul", mean[:, ch * 512:(ch + 1) * 512],
                    PS[f"psA{ch}"][0:1, :], 1.0 / D, r=(f"psA{ch}",),
                    w=(f"{pfx}.mean",), tag=f"{pfx} mean{half}{ch}")
            k.v("tensor_mul", scr[:, 0:hw], mean[:, 0:hw], mean[:, 0:hw],
                r=(f"{pfx}.mean",), w=(f"{pfx}.scr",), tag=f"{pfx} mean2{half}")
            for ch in range(hw // 512):
                k.v("scalar_tensor_tensor", scr[:, ch * 512:(ch + 1) * 512],
                    PS[f"psB{ch}"][0:1, :], 1.0 / D, scr[:, ch * 512:(ch + 1) * 512],
                    ALU.mult, ALU.subtract, r=(f"psB{ch}", f"{pfx}.scr"),
                    w=(f"{pfx}.var",), tag=f"{pfx} var{half}{ch}")
        else:
            for ch in range(hw // 512):
                k.v("tensor_scalar_mul", scr[:, ch * 512:(ch + 1) * 512],
                    PS[f"psB{ch}"][0:1, :], 1.0 / D, r=(f"psB{ch}",),
                    w=(f"{pfx}.var",), tag=f"{pfx} msq{half}{ch}")
        k.act(scr[:, 0:hw], scr[:, 0:hw], AF.Ln, bias=k.cc(eps),
              r=(f"{pfx}.var",), w=(f"{pfx}.lv",), tag=f"{pfx} ln{half}")
        k.act(rstd[:, 0:hw], scr[:, 0:hw], AF.Exp, scale=k.cc(-0.5),
              r=(f"{pfx}.lv",), w=(f"{pfx}.rstd",), tag=f"{pfx} exp{half}")
        if kind == "ln":
            k.v("tensor_copy", rows_bf[0:1, 0:hw], mean[:, 0:hw], r=(f"{pfx}.mean",),
                w=(f"{pfx}.mbf",), tag=f"{pfx} mbf{half}")
        k.v("tensor_copy", rows_bf[0:1, 1024:1024 + hw], rstd[:, 0:hw],
            r=(f"{pfx}.rstd",), w=(f"{pfx}.rbf",), tag=f"{pfx} rbf{half}")
        for ch in range(hw // 512):
            sl = slice(ch * 512, (ch + 1) * 512)
            if kind == "ln":
                k.mm(PS[f"psC{ch}"][:], k.ones_row[:], rows_bf[0:1, sl], True, True,
                     r=(f"{pfx}.mbf", "c.ones_row"), w=(f"psC{ch}",),
                     tag=f"{pfx} bcM{half}{ch}")
        if kind == "ln":
            for ch in range(hw // 512):
                k.v("tensor_copy", mbc[:, ch * 512:(ch + 1) * 512], PS[f"psC{ch}"][:],
                    r=(f"psC{ch}",), w=(f"{pfx}.mbc",), tag=f"{pfx} evM{half}{ch}")
        for ch in range(hw // 512):
            sl = slice(1024 + ch * 512, 1024 + (ch + 1) * 512)
            k.mm(PS[f"psC{ch}"][:], k.ones_row[:], rows_bf[0:1, sl], True, True,
                 r=(f"{pfx}.rbf", "c.ones_row"), w=(f"psC{ch}",),
                 tag=f"{pfx} bcR{half}{ch}")
        for ch in range(hw // 512):
            k.v("tensor_copy", rbc[:, ch * 512:(ch + 1) * 512], PS[f"psC{ch}"][:],
                r=(f"psC{ch}",), w=(f"{pfx}.rbc",), tag=f"{pfx} evR{half}{ch}")
        for ct in range(NCT):
            xlv = xln[:, ct * T + half * hw: ct * T + (half + 1) * hw]
            if kind == "ln":
                k.v("tensor_sub", xlv, xlv, mbc[:, 0:hw],
                    r=(f"{pfx}.xb{ct}", f"{pfx}.mbc"), w=(f"{pfx}.xb{ct}",),
                    tag=f"{pfx} sub{half}.{ct}")
                k.v("tensor_mul", xlv, xlv, rbc[:, 0:hw],
                    r=(f"{pfx}.xb{ct}", f"{pfx}.rbc"), w=(f"{pfx}.xb{ct}", f"{pfx}.xln"),
                    tag=f"{pfx} mul{half}.{ct}")
            else:
                gr = (gkey,) if gkey else ()
                if gain_sb is not None:
                    k.v("scalar_tensor_tensor", xlv, xlv, gain_sb[:, ct:ct + 1],
                        rbc[:, 0:hw], ALU.mult, ALU.mult,
                        r=(f"{pfx}.xb{ct}", f"{pfx}.rbc") + gr,
                        w=(f"{pfx}.xb{ct}", f"{pfx}.xln"), tag=f"{pfx} gmul{half}.{ct}")
                else:
                    k.v("tensor_mul", xlv, xlv, rbc[:, 0:hw],
                        r=(f"{pfx}.xb{ct}", f"{pfx}.rbc"),
                        w=(f"{pfx}.xb{ct}", f"{pfx}.xln"), tag=f"{pfx} mul{half}.{ct}")
    k.arena_release(_mk)
    return f"{pfx}.xln"



def emit_gemm(k, ws, x_ap, x_key, evac, pfx, banks=("psA0", "psA1", "psB0", "psB1"),
              extra_r=(), mt_list=None, preload=True):
    T, PS = k.T, k.PS
    nhalf = max(1, T // 1024)
    hw = min(T, 1024)
    mts = list(range(ws.n_m)) if mt_list is None else list(mt_list)
    for mi, mt in enumerate(mts):
        if preload:
            if mi == 0:
                ws.load(mt)
            if mi + 1 < len(mts) and ws.nbuf > 1:
                ws.load(mts[mi + 1])
        w = ws.widths[mt]
        for half in range(nhalf):
            bk0 = banks[(half % 2) * 2]
            bk1 = banks[(half % 2) * 2 + 1]
            for ct in range(ws.n_c):
                stat, wkey = ws.tile(mt, ct)
                cl = min(128, ws.C - ct * 128)
                statv = stat[0:cl, :] if cl < 128 else stat
                for ch in range(hw // 512):
                    bk = (bk0, bk1)[ch]
                    sl = slice(half * hw + ch * 512, half * hw + (ch + 1) * 512)
                    k.mm(PS[bk][0:w, :], statv, x_ap(ct, sl),
                         start=(ct == 0), stop=(ct == ws.n_c - 1),
                         r=(wkey,) + tuple(x_key(ct)) + tuple(extra_r), w=(bk,),
                         tag=f"{pfx} mm{mt}.{half}.{ct}.{ch}")
            evac(mt, half, [PS[(bk0, bk1)[ch]][0:w, :] for ch in range(hw // 512)],
                 ((bk0, bk1)[:hw // 512]))



def emit_out_resid(k, ws, y_ap, y_key, x_in, x_out, pfx,
                   banks=("psA0", "psA1", "psB0", "psB1"),
                   bias_sb=None, bias_key=None, pre=None):
    T = k.T
    hw = min(T, 1024)
    xfp = k.sb(pfx + "_xfp", [128, hw], F32)
    xo = k.sb(pfx + "_xo", [128, hw], F32)

    def evac(mt, half, ps_list, pkeys):
        kin, kout = f"{pfx}.xfp", f"{pfx}.xo"
        xf = xfp[:, 0:hw]
        xov = xo[:, 0:hw]
        k.dmac(xf, x_in.ap()[mt * 128:(mt + 1) * 128, half * hw:(half + 1) * hw],
               w=(kin,), tag=f"{pfx} ldres{mt}.{half}")
        for ch in range(hw // 512):
            src, skeys = ps_list[ch], (pkeys[ch],)
            if pre is not None:
                src, skeys = pre(mt, half, ch, src, pkeys[ch])
            if bias_sb is not None:
                k.v("scalar_tensor_tensor", xov[:, ch * 512:(ch + 1) * 512],
                    src, bias_sb[:, mt:mt + 1], xf[:, ch * 512:(ch + 1) * 512],
                    ALU.add, ALU.add,
                    r=(kin, bias_key) + tuple(skeys), w=(kout,),
                    tag=f"{pfx} add{mt}.{half}.{ch}")
            else:
                k.v("tensor_add", xov[:, ch * 512:(ch + 1) * 512],
                    xf[:, ch * 512:(ch + 1) * 512], src,
                    r=(kin,) + tuple(skeys), w=(kout,), tag=f"{pfx} add{mt}.{half}.{ch}")
        k.dmac(x_out.ap()[mt * 128:(mt + 1) * 128, half * hw:(half + 1) * hw], xov,
               r=(kout,), w=(f"xout.{pfx}.{mt}.{half}",), tag=f"{pfx} stres{mt}.{half}")

    emit_gemm(k, ws, y_ap, y_key, evac, pfx + "_og", banks=banks)
    return [f"xout.{pfx}.{mt}.{half}" for mt in range(ws.n_m) for half in range(T // hw)]



def lay_l0_attn(k, x_in, x_out, W):
    T, PS = k.T, k.PS
    NTT, NQC = T // 128, T // 512
    SC = 0.125
    xln = k.sb("l0a_xln", [128, NCT * T], BF16)
    emit_norm(k, x_in, xln, "ln", pfx="l0a")
    xap = lambda ct, sl: xln[:, ct * T + sl.start: ct * T + sl.stop]
    xkey = lambda ct: (f"l0a.xb{ct}",)

    wqS = WStream(k, "l0wq", W["wq"], W["wq_meta"])
    wkS = WStream(k, "l0wk", W["wk"], W["wk_meta"])
    wvS = WStream(k, "l0wv", W["wv"], W["wv_meta"])

    posT = k.sb("l0a_posT", [128, 256], BF16)
    k.dmacast(posT[:], W["posT2"].ap(), w=("l0a.posT",), tag="l0a posT")
    plr = k.sb("l0a_plr", [128, 2], F32)
    k.v("tensor_scalar_mul", plr[:, 0:1], posT[:, 128:129], 2.0,
        r=("l0a.posT",), w=("l0a.plr",), tag="l0a plr0")
    k.v("tensor_scalar_mul", plr[:, 1:2], posT[:, 127:128], 2.0,
        r=("l0a.posT",), w=("l0a.plr",), tag="l0a plr1")
    psq = k.sb("l0a_psq", [128, 256], BF16)
    k.v("tensor_mul", psq[0:64, :], posT[0:64, :], posT[0:64, :], r=("l0a.posT",),
        w=("l0a.psq",), tag="l0a possq")
    k.mm(PS["psB1"][0:1, 0:256], k.ones_col[0:64, :], psq[0:64, :], True, True,
         r=("l0a.psq", "c.ones_col"), w=("psB1",), tag="l0a posmm")
    mrows = k.sb("l0a_mrows", [1, 16], F32)
    k.v("reduce_max", mrows[0:1, 0:1], PS["psB1"][0:1, 0:256], axis=AX,
        r=("psB1",), w=("l0a.mp2",), tag="l0a posmax")

    vaug = k.sb("l0a_vaug", [128, 2 * NTT * 65], BF16)
    ones_ap = bass.AP(vaug, 64, [[2 * NTT * 65, 128], [65, 2 * NTT]])
    k.emit("gpsimd", lambda e: e.memset(ones_ap, 1.0), w=("l0a.vaug_ones",),
           tag="l0a vaug ones")

    QT = k.sb("l0a_QT", [128, T], BF16)
    KT = k.sb("l0a_KT", [128, T], BF16)
    KL = k.sb("l0a_KL", [128, T], BF16)
    KR = k.sb("l0a_KR", [128, T], BF16)
    vtmp = k.sb("l0a_vtmp", [128, 1024], BF16)
    sqb = k.sb("l0a_sqb", [128, T], BF16)
    negB = k.sb("l0a_negB", [1, 4], F32)
    negBc = k.sb("l0a_negBc", [128, 2], F32)
    nbf = k.sb("l0a_nbf", [1, 2], BF16)
    rel3 = k.sb("l0a_rel3", [128, 512], BF16)
    skewP = k.sb("l0a_skew", [128, 48 * 128], BF16)
    abuf = k.sb("l0a_abuf", [128, 2 * 512], BF16)
    aq = k.sb("l0a_aq", [128, 2 * 128], BF16)
    btmp = k.sb("l0a_btmp", [128, 2 * 128], F32)
    ytmp = k.sb("l0a_ytmp", [128, 512], BF16)
    YT = k.sb("l0a_YT", [128, NCT * T], BF16)
    flat = k.dram("l0a_flat", [T * 511 + 2048], BF16)
    recb = k.sb("l0a_recb", [1, 512], BF16)

    NH2 = 2

    def proj(ws, dst, p, pfx2, evac_extra=None):
        for half in range(2):
            bks = ("psA0", "psA1") if half == 0 else ("psC0", "psC1")
            for ct in range(NCT):
                stat, wkey = ws.tile(p, ct)
                for ch in range(2):
                    bk = bks[ch]
                    sl = slice(half * 1024 + ch * 512, half * 1024 + (ch + 1) * 512)
                    k.mm(PS[bk][:], stat, xap(ct, sl), start=(ct == 0), stop=(ct == NCT - 1),
                         r=(wkey,) + xkey(ct), w=(bk,), tag=f"{pfx2} mm{half}.{ct}.{ch}")
            for ch in range(2):
                bk = bks[ch]
                sl = slice(half * 1024 + ch * 512, half * 1024 + (ch + 1) * 512)
                if evac_extra is None:
                    k.v("tensor_copy", dst[:, sl], PS[bk][:], r=(bk,), w=(pfx2,),
                        tag=f"{pfx2} ev{half}.{ch}")
                else:
                    evac_extra(half, ch, bk, sl)

    for p in range(NCT):
        if p == 0:
            wqS.load(0), wkS.load(0), wvS.load(0)
        pq, pk, pv = "l0a.QT", "l0a.KT", "l0a.VT"
        proj(wqS, QT, p, pq)
        proj(wkS, KT, p, pk)

        def vevac(half, ch, bk, sl):
            k.v("tensor_copy", vtmp[:, ch * 512:(ch + 1) * 512], PS[bk][:],
                r=(bk,), w=("l0a.vtmp",), tag=f"{pv} ev{half}.{ch}")
            if ch == 1:
                for h2 in range(NH2):
                    for tt in range(8):
                        tpar = tt % 2
                        pst = PS[f"psT{tpar}"]
                        src = vtmp[h2 * 64:(h2 + 1) * 64, tt * 128:(tt + 1) * 128]
                        k.trans(pst[:, 0:64], src, k.ident(h2 * 64, 64),
                                r=("l0a.vtmp",), w=(f"psT{tpar}",),
                                tag=f"{pv} tr{half}.{h2}.{tt}")
                        kt = half * 8 + tt
                        k.v("tensor_copy",
                            vaug[:, h2 * NTT * 65 + kt * 65: h2 * NTT * 65 + kt * 65 + 64],
                            pst[:, 0:64], r=(f"psT{tpar}",), w=(f"l0a.vaug{h2}.{kt}",),
                            tag=f"{pv} trc{half}.{h2}.{tt}")
        proj(wvS, None, p, pv, evac_extra=vevac)
        if p + 1 < NCT:
            wqS.load(p + 1), wkS.load(p + 1), wvS.load(p + 1)

        k.v("tensor_scalar_add", KL[:], KT[:], plr[:, 0:1], r=(pk, "l0a.plr"),
            w=("l0a.KL",), tag=f"l0a KL{p}")
        k.v("tensor_scalar_add", KR[:], KT[:], plr[:, 1:2], r=(pk, "l0a.plr"),
            w=("l0a.KR",), tag=f"l0a KR{p}")

        for h2 in range(NH2):
            hsl = slice(h2 * 64, (h2 + 1) * 64)
            for which, buf, key in (("q", QT, pq), ("K", KT, pk)):
                k.v("tensor_mul", sqb[hsl, :], buf[hsl, :], buf[hsl, :], r=(key,),
                    w=("l0a.sqb",), tag=f"l0a sq{which}{p}.{h2}")
                for ch in range(NQC):
                    k.mm(PS["psB1"][0:1, 0:512], k.ones_col[hsl, :],
                         sqb[hsl, ch * 512:(ch + 1) * 512], True, True,
                         r=("l0a.sqb", "c.ones_col"), w=("psB1",),
                         tag=f"l0a nb mm{which}{p}.{h2}.{ch}")
                    k.v("reduce_max", mrows[0:1, 4 + ch:5 + ch], PS["psB1"][0:1, 0:512],
                        axis=AX, r=("psB1",), w=("l0a.mch",),
                        tag=f"l0a nb mx{which}{p}.{h2}.{ch}")
                k.v("reduce_max", mrows[0:1, (1 if which == "q" else 2):
                                        (2 if which == "q" else 3)],
                    mrows[0:1, 4:4 + NQC], axis=AX, r=("l0a.mch",),
                    w=(f"l0a.m{which}",), tag=f"l0a nb red{which}{p}.{h2}")
            k.v("scalar_tensor_tensor", negB[0:1, 0:1], mrows[0:1, 1:2], 3.0,
                mrows[0:1, 2:3], ALU.mult, ALU.add, r=("l0a.mq", "l0a.mK"),
                w=("l0a.b0",), tag=f"l0a b0 {p}.{h2}")
            k.v("scalar_tensor_tensor", negB[0:1, 0:1], mrows[0:1, 0:1], 2.0,
                negB[0:1, 0:1], ALU.mult, ALU.add, r=("l0a.mp2", "l0a.b0"),
                w=("l0a.b1",), tag=f"l0a b1 {p}.{h2}")
            k.v("tensor_scalar_mul", negB[0:1, 1:2], negB[0:1, 0:1], -SC * 0.5,
                r=("l0a.b1",), w=("l0a.b2",), tag=f"l0a b2 {p}.{h2}")
            k.v("tensor_copy", nbf[0:1, h2:h2 + 1], negB[0:1, 1:2], r=("l0a.b2",),
                w=("l0a.nbf",), tag=f"l0a nbf {p}.{h2}")
            k.mm(PS["psB1"][:, 0:1], k.ones_row[:], nbf[0:1, h2:h2 + 1], True, True,
                 r=("l0a.nbf", "c.ones_row"), w=("psB1",), tag=f"l0a nbc mm{p}.{h2}")
            k.v("tensor_copy", negBc[:, h2:h2 + 1], PS["psB1"][:, 0:1], r=("psB1",),
                w=(f"l0a.negBc{h2}",), tag=f"l0a nbc ev{p}.{h2}")

        for h2 in range(NH2):
            hsl = slice(h2 * 64, (h2 + 1) * 64)
            nbck = f"l0a.negBc{h2}"
            for j in range(NTT):
                k.mm(PS["psC0"][:, 0:256], QT[hsl, j * 128:(j + 1) * 128],
                     posT[hsl, :], True, True, r=(pq, "l0a.posT"), w=("psC0",),
                     tag=f"l0a rel mm{p}.{h2}.{j}")
                k.v("tensor_scalar_mul", rel3[:, 127:255], PS["psC0"][:, 128:256], 2.0,
                    r=("psC0",), w=("l0a.rel3",), tag=f"l0a r3a{p}.{h2}.{j}")
                k.v("tensor_scalar_mul", rel3[:, 255:383], PS["psC0"][:, 0:128], 2.0,
                    r=("psC0",), w=("l0a.rel3",), tag=f"l0a r3b{p}.{h2}.{j}")
                bc128 = bass.AP(PS["psC0"], 128, [[512, 128], [0, 127]])
                k.v("tensor_scalar_mul", rel3[:, 0:127], bc128, 2.0,
                    r=("psC0",), w=("l0a.rel3",), tag=f"l0a r3c{p}.{h2}.{j}")
                bc127 = bass.AP(PS["psC0"], 127, [[512, 128], [0, 128]])
                k.v("tensor_scalar_mul", rel3[:, 383:511], bc127, 2.0,
                    r=("psC0",), w=("l0a.rel3",), tag=f"l0a r3d{p}.{h2}.{j}")
                wv_ = flat.ap()[j * 128 * 511: (j * 128 + 128) * 511] \
                    .rearrange("(r c) -> r c", c=511)
                k.dmac(wv_, rel3[:, 0:511], r=("l0a.rel3",), w=(f"l0a.flat{j}",),
                       tag=f"l0a fw{p}.{h2}.{j}")
                for dk in (-1, 0, 1):
                    kt = j + dk
                    if not (0 <= kt < NTT):
                        continue
                    base = j * 128 * 511 + 255 + 128 * dk
                    rv = flat.ap()[base: base + 128 * 510] \
                        .rearrange("(r c) -> r c", c=510)
                    k.dmac(skewP[:, (j * 3 + dk + 1) * 128:(j * 3 + dk + 2) * 128],
                           rv[:, 0:128], r=(f"l0a.flat{j}",),
                           w=(f"l0a.skew{j}.{dk}",), tag=f"l0a fr{p}.{h2}.{j}.{dk}")

            for qc in range(NQC):
                qsl = slice(qc * 512, (qc + 1) * 512)
                for kt in range(NTT):
                    slot = kt % 2
                    psS = PS["psA" + str(slot)]
                    abk = f"l0a.ab{slot}"
                    av = abuf[:, slot * 512:(slot + 1) * 512]
                    segs = []
                    for g in range(4):
                        j = 4 * qc + g
                        cls = ("band" if abs(kt - j) <= 1
                               else ("KR" if kt < j - 1 else "KL"))
                        if segs and segs[-1][2] == cls:
                            segs[-1] = (segs[-1][0], g + 1, cls)
                        else:
                            segs.append((g, g + 1, cls))
                    for (g0, g1, cls) in segs:
                        c0, c1 = g0 * 128, g1 * 128
                        if cls == "band":
                            continue
                        kb = KR if cls == "KR" else KL
                        kbk = "l0a.KR" if cls == "KR" else "l0a.KL"
                        k.mm(psS[:, c0:c1], kb[hsl, kt * 128:(kt + 1) * 128],
                             QT[hsl, qc * 512 + c0: qc * 512 + c1], True, True,
                             r=(kbk, pq), w=(f"psA{slot}",),
                             tag=f"l0a S{p}.{h2}.{qc}.{kt}.{g0}")
                        k.act(av[:, c0:c1], psS[:, c0:c1], AF.Exp, bias=negBc[:, h2:h2 + 1],
                              scale=k.cc(SC), r=(f"psA{slot}", nbck), w=(abk,),
                              tag=f"l0a E{p}.{h2}.{qc}.{kt}.{g0}")
                    for (g0, g1, cls) in segs:
                        if cls != "band":
                            continue
                        for g in range(g0, g1):
                            j = 4 * qc + g
                            dk = kt - j
                            bpar = g % 2
                            psq_ap = (PS["psC0"][:, 256:384] if bpar == 0
                                      else PS["psC1"][:, 0:128])
                            psqk = "psC0" if bpar == 0 else "psC1"
                            k.mm(psq_ap, QT[hsl, j * 128:(j + 1) * 128],
                                 KT[hsl, kt * 128:(kt + 1) * 128], True, True,
                                 r=(pq, pk), w=(psqk,),
                                 tag=f"l0a bq{p}.{h2}.{qc}.{kt}.{g}")
                            k.v("tensor_add", btmp[:, bpar * 128:(bpar + 1) * 128],
                                psq_ap, skewP[:, (j * 3 + dk + 1) * 128:(j * 3 + dk + 2) * 128],
                                r=(psqk, f"l0a.skew{j}.{dk}"), w=(f"l0a.btmp{bpar}",),
                                tag=f"l0a ba{p}.{h2}.{qc}.{kt}.{g}")
                            k.act(aq[:, bpar * 128:(bpar + 1) * 128],
                                  btmp[:, bpar * 128:(bpar + 1) * 128], AF.Exp,
                                  bias=negBc[:, h2:h2 + 1], scale=k.cc(SC),
                                  r=(f"l0a.btmp{bpar}", nbck), w=(f"l0a.aq{bpar}",),
                                  tag=f"l0a be{p}.{h2}.{qc}.{kt}.{g}")
                            pst = PS[f"psT{bpar}"]
                            k.trans(pst[:, 0:128], aq[:, bpar * 128:(bpar + 1) * 128],
                                    k.ident(0, 128), r=(f"l0a.aq{bpar}",),
                                    w=(f"psT{bpar}",),
                                    tag=f"l0a btr{p}.{h2}.{qc}.{kt}.{g}")
                            k.v("tensor_copy", av[:, g * 128:(g + 1) * 128],
                                pst[:, 0:128], r=(f"psT{bpar}",), w=(abk,),
                                tag=f"l0a btc{p}.{h2}.{qc}.{kt}.{g}")
                    k.mm(PS["psB0"][0:65, :],
                         vaug[:, h2 * NTT * 65 + kt * 65: h2 * NTT * 65 + (kt + 1) * 65],
                         av, start=(kt == 0), stop=(kt == NTT - 1),
                         r=(abk, f"l0a.vaug{h2}.{kt}", "l0a.vaug_ones"),
                         w=("psB0",), tag=f"l0a av{p}.{h2}.{qc}.{kt}")
                k.v("reciprocal", recb[0:1, :], PS["psB0"][64:65, :], r=("psB0",),
                    w=("l0a.recb",), tag=f"l0a rc{p}.{h2}.{qc}", lowp=True)
                k.mm(PS["psB1"][0:64, :], k.ones_row[0:1, 0:64], recb[0:1, :],
                     True, True, r=("l0a.recb", "c.ones_row"), w=("psB1",),
                     tag=f"l0a rb{p}.{h2}.{qc}")
                k.act(ytmp[0:64, :], PS["psB0"][0:64, :], AF.Copy, r=("psB0",),
                      w=("l0a.ytmp",), tag=f"l0a yc{p}.{h2}.{qc}")
                if h2 == 0:
                    k.v("tensor_mul", YT[0:64, p * T + qc * 512:p * T + (qc + 1) * 512],
                        ytmp[0:64, :], PS["psB1"][0:64, :],
                        r=("l0a.ytmp", "psB1"), w=("l0a.yt",),
                        tag=f"l0a ym{p}.{h2}.{qc}")
                else:
                    k.v("tensor_mul", ytmp[0:64, :], ytmp[0:64, :], PS["psB1"][0:64, :],
                        r=("l0a.ytmp", "psB1"), w=("l0a.ytmp",),
                        tag=f"l0a ym{p}.{h2}.{qc}")
                    k.dmac(YT[64:128, p * T + qc * 512:p * T + (qc + 1) * 512],
                           ytmp[0:64, :], r=("l0a.ytmp",), w=("l0a.yt",),
                           tag=f"l0a ymv{p}.{h2}.{qc}")

    wpS = WStream(k, "l0wp", W["wp"], W["wp_meta"])
    outk = emit_out_resid(k, wpS, lambda ct, sl: YT[:, ct * T + sl.start: ct * T + sl.stop],
                          lambda ct: ("l0a.yt",), x_in, x_out, "l0a")
    return outk



def prep_l0_attn(inputs):
    arrs, meta = {}, {}
    for nm, w in (("wq", inputs["L0_attn_Wq"]), ("wk", inputs["L0_attn_Wk"]),
                  ("wv", inputs["L0_attn_Wv"]), ("wp", inputs["L0_attn_Wp"])):
        arr, m = host_tiles(np.asarray(w, np.float32).T)
        arrs["l0a_" + nm] = arr
        meta[nm + "_meta"] = m
    pos = np.asarray(inputs["L0_attn_pos_emb_k"], np.float32).T
    arrs["l0a_posT2"] = np.ascontiguousarray(np.concatenate([pos, pos], axis=0))
    return arrs, meta


def emit_l0_attn(k, x_in, x_out, W):
    return lay_l0_attn(k, x_in, x_out,
                       {"wq": W["l0a_wq"], "wq_meta": W["wq_meta"],
                        "wk": W["l0a_wk"], "wk_meta": W["wk_meta"],
                        "wv": W["l0a_wv"], "wv_meta": W["wv_meta"],
                        "wp": W["l0a_wp"], "wp_meta": W["wp_meta"],
                        "posT2": W["l0a_posT2"]})





def finish(k, out_keys):
    k.emit("sync", lambda e: e.nop(), r=tuple(out_keys), tag="final wait")


def build_layer_program(names, T=2048, prepped=None):
    nc = bass.Bass("TRN2", target_bir_lowering=False, num_devices=NCORES)
    k = K(nc, T=T)
    k.setup_common()
    x_in = k.dram("xin", [D, T], F32, kind="ExternalInput")
    x_out = k.dram("xout", [D, T], F32, kind="ExternalOutput")
    bufs = [x_in]
    for i in range(1, len(names)):
        bufs.append(k.dram(f"xmid{i}", [D, T], F32, kind="Internal"))
    bufs.append(x_out)
    all_keys = []
    for i, name in enumerate(names):
        prep, emitf = None, None
        for (nm, p, e) in LAYERS:
            if nm == name:
                prep, emitf = p, e
        arrs, meta = prepped[name]
        W = dict(meta)
        for an, arr in arrs.items():
            shp = list(arr.shape[1:]) if an.startswith("pc8_") else list(arr.shape)
            dt = k.dram(an, shp, F32, kind="ExternalInput")
            W[an] = dt
        if i > 0:
            k.arena_reset()
        keys = emitf(k, bufs[i], bufs[i + 1], W)
        all_keys = keys
    finish(k, all_keys)
    k.build()
    return k


def run_layers(names, x_full, inputs, T=2048, sim=False, cores=None):
    prepped = {nm: dict(LAYERS)[nm] if False else None for nm in names}
    prepped = {}
    for nm in names:
        for (n2, p, e) in LAYERS:
            if n2 == nm:
                prepped[nm] = p(inputs)
    k = build_layer_program(names, T=T, prepped=prepped)
    nc = k.nc
    nc.finalize()
    B = x_full.shape[0]
    cores = list(range(B)) if cores is None else cores
    in_maps = []
    ident = np.eye(128, dtype=np.float32)
    for b in range(len(cores)):
        m = {"xin": np.ascontiguousarray(x_full[b].T.astype(np.float32)),
             "c_ident_f32": ident}
        for nm in names:
            for an, arr in prepped[nm][0].items():
                m[an] = arr[b] if an.startswith("pc8_") else arr
        in_maps.append(m)
    if sim:
        from concourse.bass_interp import MultiCoreSim
        s = MultiCoreSim(nc, num_cores=len(cores), num_workers=1, trace=False)
        for i in range(len(cores)):
            for kk, vv in in_maps[i].items():
                s.cores[i].tensor(kk)[:] = vv
        s.simulate(check_with_hw=False)
        outs = [np.array(s.cores[i].tensor("xout")) for i in range(len(cores))]
    else:
        res = _bass_utils.run_bass_kernel_spmd(nc, in_maps, core_ids=list(range(len(cores))))
        outs = [r["xout"] for r in res.results]
    return np.stack([o.T for o in outs])



def prep_l3_ffn(inputs):
    arrs, meta = {}, {}
    for nm, w in (("wr", inputs["L3_ffn_W_r"]), ("wk", inputs["L3_ffn_W_k"]),
                  ("wv", inputs["L3_ffn_W_v"])):
        arr, m = host_tiles(np.asarray(w, np.float32).T)
        arrs["l3f_" + nm] = arr
        meta[nm + "_meta"] = m
    mu_r = np.asarray(inputs["L3_ffn_mu_r"], np.float32).reshape(NCT, 128).T
    mu_k = np.asarray(inputs["L3_ffn_mu_k"], np.float32).reshape(NCT, 128).T
    arrs["l3f_mu"] = np.ascontiguousarray(np.concatenate([mu_r, mu_k], axis=1))
    return arrs, meta


def emit_l3_ffn(k, x_in, x_out, W):
    T = k.T
    xln = k.sb("l3f_xln", [128, NCT * T], BF16)
    emit_norm(k, x_in, xln, "ln", pfx="l3f")
    mu = k.sb("l3f_mu", [128, 2 * NCT], F32)
    k.dmac(mu[:], W["l3f_mu"].ap(), w=("l3f.mu",), tag="l3f mu")
    xr = k.sb("l3f_xr", [128, NCT * T], BF16)
    xk = k.sb("l3f_xk", [128, NCT * T], BF16)
    tmp = k.sb("l3f_tmp", [128, T], BF16)
    for ct in range(NCT):
        xv = xln[:, ct * T:(ct + 1) * T]
        for (dst, kdst, mcol) in ((xr, "l3f.xr", ct), (xk, "l3f.xk", NCT + ct)):
            dv = dst[:, ct * T:(ct + 1) * T]
            k.v("tensor_sub", tmp[:, 1:T], xv[:, 1:T], xv[:, 0:T - 1],
                r=(f"l3f.xb{ct}",), w=("l3f.tmp",), tag=f"l3f d{kdst}{ct}")
            k.v("scalar_tensor_tensor", dv[:, 1:T], tmp[:, 1:T], mu[:, mcol:mcol + 1],
                xv[:, 0:T - 1], ALU.mult, ALU.add,
                r=("l3f.tmp", "l3f.mu", f"l3f.xb{ct}"), w=(kdst,),
                tag=f"l3f m{kdst}{ct}")
            k.v("tensor_scalar_mul", dv[:, 0:1], xv[:, 0:1], mu[:, mcol:mcol + 1],
                r=("l3f.mu", f"l3f.xb{ct}"), w=(kdst,), tag=f"l3f e{kdst}{ct}")
    rT = k.sb("l3f_rT", [128, NCT * T], BF16)
    kaT = xln
    krelu = k.sb("l3f_krelu", [128, 1024], F32)
    wrS = WStream(k, "l3fwr", W["l3f_wr"], W["wr_meta"])
    wkS = WStream(k, "l3fwk", W["l3f_wk"], W["wk_meta"])
    wvS = WStream(k, "l3fwv", W["l3f_wv"], W["wv_meta"])

    def revac(mt, half, ps_list, pkeys):
        for ch in range(2):
            sl = slice(mt * T + half * 1024 + ch * 512, mt * T + half * 1024 + (ch + 1) * 512)
            k.act(rT[:, sl], ps_list[ch], AF.Sigmoid, r=(pkeys[ch],), w=("l3f.rT",),
                  tag=f"l3f rs{mt}.{half}.{ch}")
    emit_gemm(k, wrS, lambda ct, sl: xr[:, ct * T + sl.start:ct * T + sl.stop],
              lambda ct: ("l3f.xr",), revac, "l3f_r")

    def kevac(mt, half, ps_list, pkeys):
        for ch in range(2):
            sl = slice(mt * T + half * 1024 + ch * 512, mt * T + half * 1024 + (ch + 1) * 512)
            kr = krelu[:, ch * 512:(ch + 1) * 512]
            k.v("tensor_scalar_max", kr, ps_list[ch], 0.0, r=(pkeys[ch],),
                w=("l3f.krelu",), tag=f"l3f kr{mt}.{half}.{ch}")
            k.act(kaT[:, sl], kr, AF.Square,
                  r=("l3f.krelu", "l3f.xr", "l3f.xk"), w=("l3f.kaT",),
                  tag=f"l3f kq{mt}.{half}.{ch}")
    emit_gemm(k, wkS, lambda ct, sl: xk[:, ct * T + sl.start:ct * T + sl.stop],
              lambda ct: ("l3f.xk",), kevac, "l3f_k")

    def pre(mt, half, ch, ps, pkey):
        sl = slice(mt * T + half * 1024 + ch * 512, mt * T + half * 1024 + (ch + 1) * 512)
        ot = tmp[:, ch * 512:(ch + 1) * 512]
        k.v("tensor_mul", ot, rT[:, sl], ps, r=("l3f.rT", pkey), w=("l3f.tmp",),
            tag=f"l3f pm{mt}.{half}.{ch}")
        return ot, ("l3f.tmp",)
    return emit_out_resid(k, wvS,
                          lambda ct, sl: kaT[:, ct * T + sl.start:ct * T + sl.stop],
                          lambda ct: ("l3f.kaT",), x_in, x_out, "l3f", pre=pre)



def prep_l1_attn(inputs):
    arrs, meta = {}, {}
    arr, m = host_tiles(np.asarray(inputs["L1_attn_qkvu_w"], np.float32).T)
    arrs["l1a_wqkvu"] = arr
    meta["wqkvu_meta"] = m
    arr, m = host_tiles(np.asarray(inputs["L1_attn_out_w"], np.float32).T)
    arrs["l1a_wo"] = arr
    meta["wo_meta"] = m
    b = np.asarray(inputs["L1_attn_qkvu_b"], np.float32).reshape(32, 128).T
    arrs["l1a_bqkvu"] = np.ascontiguousarray(b)
    arrs["l1a_bo"] = np.ascontiguousarray(
        np.asarray(inputs["L1_attn_out_b"], np.float32).reshape(8, 128).T)
    arrs["l1a_theta"] = np.ascontiguousarray(
        np.asarray(inputs["L1_attn_theta"], np.float32).reshape(1, 1024))
    arrs["l1a_slope"] = np.ascontiguousarray(
        np.asarray(inputs["L1_attn_slope_rate"], np.float32).reshape(1, 8))
    arrs["c_iota_t"] = np.arange(2048, dtype=np.float32).reshape(1, 2048)
    arrs["c_iota128"] = np.arange(128, dtype=np.float32).reshape(1, 128)
    arrs["c_iota16h"] = (np.arange(16, dtype=np.float32) * 128).reshape(1, 16)
    tri = np.zeros((128, 4 * 512), np.float32)
    for g in range(4):
        p = np.arange(128)[:, None]
        c = np.arange(512)[None, :]
        tri[:, g * 512:(g + 1) * 512] = (c >= (p + 128 * g)).astype(np.float32)
    arrs["c_tri4"] = tri
    return arrs, meta


def emit_l1_attn(k, x_in, x_out, W):
    T, PS = k.T, k.PS
    NTT, NQC = T // 128, T // 512
    xln = k.sb("l1a_xln", [128, NCT * T], BF16)
    emit_norm(k, x_in, xln, "ln", pfx="l1a")
    xap = lambda ct, sl: xln[:, ct * T + sl.start: ct * T + sl.stop]
    xkey = lambda ct: (f"l1a.xb{ct}",)

    wS = WStream(k, "l1aw", W["l1a_wqkvu"], W["wqkvu_meta"], nbuf=2)
    bq = k.sb("l1a_bq", [128, 32], F32)
    k.dmac(bq[:], W["l1a_bqkvu"].ap(), w=("l1a.bq",), tag="l1a bq")
    tri4 = k.sb("c_tri4", [128, 4 * 512], BF16)
    k.dmacast(tri4[:], W["c_tri4"].ap(), w=("c.tri4",), tag="l1a tri4")
    iota128 = k.sb("c_iota128", [1, 128], F32)
    k.dmac(iota128[:], W["c_iota128"].ap(), w=("c.iota128",), tag="l1a iota128")
    iota16h = k.sb("c_iota16h", [1, 16], F32)
    k.dmac(iota16h[:], W["c_iota16h"].ap(), w=("c.iota16",), tag="l1a iota16")
    theta = k.sb("l1a_theta", [1, 1024], F32)
    k.dmac(theta[:], W["l1a_theta"].ap(), w=("l1a.theta",), tag="l1a theta")
    slope = k.sb("l1a_slope", [1, 8], F32)
    k.dmac(slope[:], W["l1a_slope"].ap(), w=("l1a.slope",), tag="l1a slope")
    es_row = k.sb("l1a_es", [1, 8], F32)
    es_bf = k.sb("l1a_esb", [1, 8], BF16)
    es_col = k.sb("l1a_esc", [128, 8], F32)
    k.act(es_row[:], slope[:], AF.Exp, r=("l1a.slope",), w=("l1a.es",), tag="l1a es")
    k.v("tensor_copy", es_bf[:], es_row[:], r=("l1a.es",), w=("l1a.esb",), tag="l1a esb")
    k.mm(PS["psB1"][:, 0:8], k.ones_row[:], es_bf[:], True, True,
         r=("l1a.esb", "c.ones_row"), w=("psB1",), tag="l1a esmm")
    k.v("tensor_copy", es_col[:], PS["psB1"][:, 0:8], r=("psB1",), w=("l1a.esc",),
        tag="l1a esev")

    QT = k.sb("l1a_QT", [128, T], BF16)
    KT = k.sb("l1a_KT", [128, T], BF16)
    qlc = k.sb("l1a_qlc", [128, T], BF16)
    qls = k.sb("l1a_qls", [128, T], BF16)
    klc = k.sb("l1a_klc", [128, T], BF16)
    kls = k.sb("l1a_kls", [128, T], BF16)
    SIN = k.sb("l1a_SIN", [128, T], BF16)
    COS = k.sb("l1a_COS", [128, T], BF16)
    wrk = k.sb("l1a_wrk", [128, 1024], F32)
    vtmp = k.sb("l1a_vtmp", [128, 1024], BF16)
    VV = k.sb("l1a_V", [128, NTT * 128], BF16)
    ebuf = k.sb("l1a_eb", [128, 4 * 512], BF16)
    y1tmp = k.sb("l1a_y1", [128, 512], BF16)
    YT = k.sb("l1a_YT", [128, NCT * T], BF16)
    UT = k.sb("l1a_UT", [128, NCT * T], BF16)

    AB = k.sb("l1a_AB", [128, 2 * 144], F32)
    wred = k.sb("l1a_wred", [128, 144], F32)

    def wrap_reduce(x_ap, n, kmax, key):
        twopi = 2 * np.pi
        ge = wred[:, 0:n]
        for kk in range(kmax, -1, -1):
            j = twopi * (2 ** kk)
            k.v("tensor_single_scalar", ge, x_ap, float(j), ALU.is_ge,
                r=(key,), w=("l1a.ge",), tag=f"l1a rr{kk}")
            k.v("scalar_tensor_tensor", x_ap, ge, float(-j), x_ap, ALU.mult, ALU.add,
                r=(key, "l1a.ge"), w=(key,), tag=f"l1a rs{kk}")
        k.v("tensor_single_scalar", ge, x_ap, float(np.pi), ALU.is_ge,
            r=(key,), w=("l1a.ge",), tag="l1a rf0")
        k.v("scalar_tensor_tensor", x_ap, ge, float(-twopi), x_ap, ALU.mult, ALU.add,
            r=(key, "l1a.ge"), w=(key,), tag="l1a rf1")

    def sincos_of(x_ap, n, sin_ap, cos_ap, key):
        k.act(sin_ap, x_ap, AF.Sin, r=(key,), w=("l1a.sc",), tag="l1a sS")
        ge = wred[:, 0:n]
        k.v("tensor_single_scalar", ge, x_ap, float(np.pi / 2), ALU.is_ge,
            r=(key,), w=("l1a.ge",), tag="l1a cg")
        k.v("scalar_tensor_tensor", ge, ge, float(-2 * np.pi), x_ap,
            ALU.mult, ALU.add, r=(key, "l1a.ge"), w=("l1a.ge",), tag="l1a cs")
        k.v("tensor_scalar_add", cos_ap, ge, float(np.pi / 2), r=("l1a.ge",),
            w=("l1a.cw",), tag="l1a ca")
        k.act(cos_ap, cos_ap, AF.Sin, r=("l1a.cw",), w=("l1a.sc",), tag="l1a cS")

    def projmm(mt, b, dst_evac):
        for half in range(2):
            bks = ("psA0", "psA1") if half == 0 else ("psC0", "psC1")
            for ct in range(NCT):
                stat, wkey = wS.tile(mt, ct, b=b)
                for ch in range(2):
                    sl = slice(half * 1024 + ch * 512, half * 1024 + (ch + 1) * 512)
                    k.mm(PS[bks[ch]][:], stat, xap(ct, sl),
                         start=(ct == 0), stop=(ct == NCT - 1),
                         r=(wkey,) + xkey(ct), w=(bks[ch],), tag=f"l1a mm{mt}.{half}.{ch}")
            for ch in range(2):
                sl = slice(half * 1024 + ch * 512, half * 1024 + (ch + 1) * 512)
                dst_evac(sl, PS[bks[ch]][:], bks[ch])

    wS.load(0, b=0)
    for h in range(8):
        thr = theta[0:1, h * 128:(h + 1) * 128]
        k.mm(PS["psB0"][:, 0:16], thr, iota16h[:], True, True,
             r=("l1a.theta", "c.iota16"), w=("psB0",), tag=f"l1a thA{h}")
        k.mm(PS["psB1"][:, 0:128], thr, iota128[:], True, True,
             r=("l1a.theta", "c.iota128"), w=("psB1",), tag=f"l1a thB{h}")
        A_ = wrk[:, 0:16]
        B_ = wrk[:, 16:144]
        k.v("tensor_copy", A_, PS["psB0"][:, 0:16], r=("psB0",), w=("l1a.thA",),
            tag=f"l1a thAc{h}")
        k.v("tensor_copy", B_, PS["psB1"][:, 0:128], r=("psB1",), w=("l1a.thB",),
            tag=f"l1a thBc{h}")
        wrap_reduce(A_, 16, 10, "l1a.thA")
        wrap_reduce(B_, 128, 6, "l1a.thB")
        sA, cA = AB[:, 0:16], AB[:, 16:32]
        sB, cB = AB[:, 32:160], AB[:, 160:288]
        sincos_of(A_, 16, sA, cA, "l1a.thA")
        sincos_of(B_, 128, sB, cB, "l1a.thB")
        for tt in range(NTT):
            csl = slice(tt * 128, (tt + 1) * 128)
            k.v("scalar_tensor_tensor", wrk[:, 144:272], cB, sA[:, tt:tt + 1], sB,
                ALU.mult, ALU.bypass, r=("l1a.sc",), w=("l1a.t1",),
                tag=f"l1a sc1{h}.{tt}")
            k.v("scalar_tensor_tensor", SIN[:, csl], sB, cA[:, tt:tt + 1],
                wrk[:, 144:272], ALU.mult, ALU.add, r=("l1a.sc", "l1a.t1"),
                w=("l1a.SIN",), tag=f"l1a sc2{h}.{tt}")
            k.v("scalar_tensor_tensor", wrk[:, 144:272], sB, sA[:, tt:tt + 1], sB,
                ALU.mult, ALU.bypass, r=("l1a.sc",), w=("l1a.t1",),
                tag=f"l1a sc3{h}.{tt}")
            k.v("scalar_tensor_tensor", COS[:, csl], cB, cA[:, tt:tt + 1],
                wrk[:, 144:272], ALU.mult, ALU.subtract, r=("l1a.sc", "l1a.t1"),
                w=("l1a.COS",), tag=f"l1a sc4{h}.{tt}")
        wS.load(8 + h, b=1)
        def qevac(sl, ps, pk):
            k.act(QT[:, sl], ps, AF.Silu, bias=bq[:, h:h + 1],
                  r=(pk, "l1a.bq"), w=("l1a.QT",), tag=f"l1a qe{h}.{sl.start}")
        def kevac(sl, ps, pk):
            k.act(KT[:, sl], ps, AF.Silu, bias=bq[:, 8 + h:9 + h],
                  r=(pk, "l1a.bq"), w=("l1a.KT",), tag=f"l1a ke{h}.{sl.start}")
        def uevac(sl, ps, pk):
            k.v("tensor_scalar_add", UT[:, h * T + sl.start:h * T + sl.stop], ps,
                bq[:, 24 + h:25 + h], r=(pk, "l1a.bq"), w=("l1a.UT",),
                tag=f"l1a ue{h}.{sl.start}")
        def vevac(sl, ps, pk):
            half = sl.start // 1024
            ch = (sl.start % 1024) // 512
            k.v("tensor_scalar_add", vtmp[:, ch * 512:(ch + 1) * 512], ps,
                bq[:, 16 + h:17 + h], r=(pk, "l1a.bq"), w=("l1a.vtmp",),
                tag=f"l1a ve{h}.{sl.start}")
            if ch == 1:
                for tt in range(8):
                    tpar = tt % 2
                    pst = PS[f"psT{tpar}"]
                    k.trans(pst[:, 0:128], vtmp[:, tt * 128:(tt + 1) * 128],
                            k.ident(0, 128), r=("l1a.vtmp",), w=(f"psT{tpar}",),
                            tag=f"l1a vt{h}.{half}.{tt}")
                    kt = half * 8 + tt
                    k.v("tensor_copy", VV[:, kt * 128:(kt + 1) * 128], pst[:, 0:128],
                        r=(f"psT{tpar}",), w=(f"l1a.V{kt}",), tag=f"l1a vc{h}.{half}.{tt}")
        projmm(h, 0, qevac)
        wS.load(16 + h, b=0)
        projmm(8 + h, 1, kevac)
        wS.load(24 + h, b=1)
        projmm(16 + h, 0, vevac)
        if h + 1 < 8:
            wS.load(h + 1, b=0)
        projmm(24 + h, 1, uevac)
        for (dst, kd, src, ks2, tbl, kt3) in (
                (qlc, "l1a.qlc", QT, "l1a.QT", COS, "l1a.COS"),
                (qls, "l1a.qls", QT, "l1a.QT", SIN, "l1a.SIN"),
                (klc, "l1a.klc", KT, "l1a.KT", COS, "l1a.COS"),
                (kls, "l1a.kls", KT, "l1a.KT", SIN, "l1a.SIN")):
            k.v("tensor_mul", dst[:], src[:], tbl[:], r=(ks2, kt3), w=(kd,),
                tag=f"l1a lr{kd}{h}")
        for qc in range(NQC):
            for kt in range(NTT):
                slot = kt % 2
                psS = PS["psA" + str(slot)]
                k.mm(psS[:], klc[:, kt * 128:(kt + 1) * 128],
                     qlc[:, qc * 512:(qc + 1) * 512], True, False,
                     r=("l1a.klc", "l1a.qlc"), w=(f"psA{slot}",),
                     tag=f"l1a E1{h}.{qc}.{kt}")
                k.mm(psS[:], kls[:, kt * 128:(kt + 1) * 128],
                     qls[:, qc * 512:(qc + 1) * 512], False, True,
                     r=("l1a.kls", "l1a.qls"), w=(f"psA{slot}",),
                     tag=f"l1a E2{h}.{qc}.{kt}")
                g = kt - 4 * qc
                eb_lo = ebuf[:, slot * 1024:slot * 1024 + 512]
                eb_hi = ebuf[:, slot * 1024 + 512:slot * 1024 + 1024]
                klo, khi = f"l1a.el{slot}", f"l1a.eh{slot}"
                y1_last = min(4 * qc + 3, NTT - 1)
                if g < 0:
                    k.v("tensor_copy", eb_lo, psS[:], r=(f"psA{slot}",), w=(klo,),
                        tag=f"l1a ec{h}.{qc}.{kt}")
                    k.mm(PS["psB0"][:], VV[:, kt * 128:(kt + 1) * 128], eb_lo,
                         start=(kt == 0), stop=(kt == y1_last),
                         r=(klo, f"l1a.V{kt}"), w=("psB0",), tag=f"l1a y1{h}.{qc}.{kt}")
                elif g > 3:
                    k.v("tensor_copy", eb_hi, psS[:], r=(f"psA{slot}",), w=(khi,),
                        tag=f"l1a ec{h}.{qc}.{kt}")
                    k.mm(PS["psB1"][:], VV[:, kt * 128:(kt + 1) * 128], eb_hi,
                         start=(kt == 4 * qc), stop=(kt == NTT - 1),
                         r=(khi, f"l1a.V{kt}"), w=("psB1",), tag=f"l1a y2{h}.{qc}.{kt}")
                else:
                    k.v("tensor_mul", eb_lo, psS[:], tri4[:, g * 512:(g + 1) * 512],
                        r=(f"psA{slot}", "c.tri4"), w=(klo,), tag=f"l1a em{h}.{qc}.{kt}")
                    k.v("scalar_tensor_tensor", eb_hi, eb_lo, -1.0, psS[:],
                        ALU.mult, ALU.add, r=(klo, f"psA{slot}"), w=(khi,),
                        tag=f"l1a es{h}.{qc}.{kt}")
                    k.mm(PS["psB0"][:], VV[:, kt * 128:(kt + 1) * 128], eb_lo,
                         start=(kt == 0), stop=(kt == y1_last),
                         r=(klo, f"l1a.V{kt}"), w=("psB0",), tag=f"l1a y1{h}.{qc}.{kt}")
                    k.mm(PS["psB1"][:], VV[:, kt * 128:(kt + 1) * 128], eb_hi,
                         start=(kt == 4 * qc), stop=(kt == NTT - 1),
                         r=(khi, f"l1a.V{kt}"), w=("psB1",), tag=f"l1a y2{h}.{qc}.{kt}")
            k.act(y1tmp[:], PS["psB0"][:], AF.Copy, r=("psB0",), w=("l1a.y1t",),
                  tag=f"l1a yc{h}.{qc}")
            k.v("scalar_tensor_tensor", YT[:, h * T + qc * 512:h * T + (qc + 1) * 512],
                y1tmp[:], es_col[:, h:h + 1], PS["psB1"][:], ALU.mult, ALU.add,
                r=("l1a.y1t", "l1a.esc", "psB1"), w=("l1a.yt",),
                tag=f"l1a yf{h}.{qc}")

    emit_srms_u(k, YT, UT, "l1a")

    woS = WStream(k, "l1awo", W["l1a_wo"], W["wo_meta"])
    bo = k.sb("l1a_bo", [128, 8], F32)
    k.dmac(bo[:], W["l1a_bo"].ap(), w=("l1a.bo",), tag="l1a bo")
    return emit_out_resid(k, woS,
                          lambda ct, sl: YT[:, ct * T + sl.start: ct * T + sl.stop],
                          lambda ct: ("l1a.ytn",), x_in, x_out, "l1a",
                          bias_sb=bo, bias_key="l1a.bo")


def emit_srms_u(k, YT, UT, pfx, eps=1e-6):
    T, PS = k.T, k.PS
    _mk = k.arena_mark()
    sq = k.sb(pfx + "_nsq", [128, 1024], BF16)
    rows = k.sb(pfx + "_nrows", [1, 1024], F32)
    rows_bf = k.sb(pfx + "_nrowsb", [1, 1024], BF16)
    rbc = k.sb(pfx + "_nrbc", [128, 1024], BF16)
    for half in range(T // 1024):
        hs = slice(half * 1024, (half + 1) * 1024)
        for ct in range(NCT):
            ylv = YT[:, ct * T + half * 1024: ct * T + (half + 1) * 1024]
            k.v("tensor_mul", sq[:], ylv, ylv, r=("l1a.yt",) if pfx == "l1a" else (f"{pfx}.yt",),
                w=(f"{pfx}.nsq",), tag=f"{pfx} nsq{half}.{ct}")
            for ch in range(2):
                k.mm(PS["psB" + str(ch)][0:1, :], k.ones_col[:],
                     sq[:, ch * 512:(ch + 1) * 512],
                     start=(ct == 0), stop=(ct == NCT - 1),
                     r=(f"{pfx}.nsq", "c.ones_col"), w=(f"psB{ch}",),
                     tag=f"{pfx} nmm{half}.{ct}.{ch}")
        scr = rows[0:1, 0:1024]
        for ch in range(2):
            k.v("tensor_scalar_mul", scr[:, ch * 512:(ch + 1) * 512],
                PS[f"psB{ch}"][0:1, :], 1.0 / D, r=(f"psB{ch}",), w=(f"{pfx}.nvar",),
                tag=f"{pfx} nms{half}.{ch}")
        k.act(scr, scr, AF.Ln, bias=k.cc(eps), r=(f"{pfx}.nvar",), w=(f"{pfx}.nlv",),
              tag=f"{pfx} nln{half}")
        k.act(scr, scr, AF.Exp, scale=k.cc(-0.5), r=(f"{pfx}.nlv",),
              w=(f"{pfx}.nrstd",), tag=f"{pfx} nexp{half}")
        k.v("tensor_copy", rows_bf[:], scr, r=(f"{pfx}.nrstd",),
            w=(f"{pfx}.nrbf",), tag=f"{pfx} nrbf{half}")
        for ch in range(2):
            k.mm(PS[f"psC{ch}"][:], k.ones_row[:], rows_bf[0:1, ch * 512:(ch + 1) * 512],
                 True, True, r=(f"{pfx}.nrbf", "c.ones_row"), w=(f"psC{ch}",),
                 tag=f"{pfx} nbc{half}.{ch}")
            k.v("tensor_copy", rbc[:, ch * 512:(ch + 1) * 512], PS[f"psC{ch}"][:],
                r=(f"psC{ch}",), w=(f"{pfx}.nrbc",), tag=f"{pfx} nev{half}.{ch}")
        for ct in range(NCT):
            ylv = YT[:, ct * T + half * 1024: ct * T + (half + 1) * 1024]
            ulv = UT[:, ct * T + half * 1024: ct * T + (half + 1) * 1024]
            k.v("tensor_mul", ylv, ylv, rbc[:],
                r=(("l1a.yt" if pfx == "l1a" else f"{pfx}.yt"), f"{pfx}.nrbc"),
                w=(f"{pfx}.ytn0",), tag=f"{pfx} nm1{half}.{ct}")
            k.v("tensor_mul", ylv, ylv, ulv,
                r=(f"{pfx}.ytn0", "l1a.UT" if pfx == "l1a" else f"{pfx}.UT"),
                w=(f"{pfx}.ytn",), tag=f"{pfx} nm2{half}.{ct}")
    k.arena_release(_mk)






def prep_l0_ffn(inputs):
    arrs, meta = {}, {}
    for nm, w in (("wq", inputs["L0_ffn_Wq"]), ("wdkv", inputs["L0_ffn_Wdkv"]),
                  ("wuk", inputs["L0_ffn_Wuk"]), ("wuv", inputs["L0_ffn_Wuv"]),
                  ("wo", inputs["L0_ffn_Wo"]), ("gate", inputs["L0_ffn_gate_w"])):
        arr, m = host_tiles(np.asarray(w, np.float32))
        arrs["l0f_" + nm] = arr
        meta[nm + "_meta"] = m
    w1 = np.asarray(inputs["L0_ffn_w1"], np.float32)
    w2 = np.asarray(inputs["L0_ffn_w2"], np.float32)
    p1 = [host_tiles(w1[e]) for e in range(8)]
    p2 = [host_tiles(w2[e]) for e in range(8)]
    arrs["l0f_w1"] = np.concatenate([p[0] for p in p1])
    arrs["l0f_w2"] = np.concatenate([p[0] for p in p2])
    m1 = p1[0][1]
    m2 = p2[0][1]
    meta["w1_meta"] = (m1[0] * 8, m1[1], m1[2] * 8, m1[3])
    meta["w2_meta"] = (m2[0] * 8, m2[1], m2[2] * 8, m2[3])
    g1 = np.asarray(inputs["L0_ffn_g1"], np.float32).reshape(NCT, 128).T
    g2 = np.asarray(inputs["L0_ffn_g2"], np.float32).reshape(NCT, 128).T
    arrs["l0f_g"] = np.ascontiguousarray(np.concatenate([g1, g2], axis=1))
    tri = np.zeros((128, 4 * 512), np.float32)
    for g in range(4):
        p = np.arange(128)[:, None]
        c = np.arange(512)[None, :]
        tri[:, g * 512:(g + 1) * 512] = (c >= (p + 128 * g)).astype(np.float32)
    arrs["c_tri4"] = tri
    return arrs, meta


def emit_l0_ffn(k, x_in, x_out, W):
    T, PS = k.T, k.PS
    NTT, NQC = T // 128, T // 512
    SC = 0.125
    R = k.sb("l0f_R", [128, NCT * T], BF16)
    emit_norm(k, x_in, R, "ln", pfx="l0f")
    gg = k.sb("l0f_g", [128, 2 * NCT], F32)
    k.dmac(gg[:], W["l0f_g"].ap(), w=("l0f.g",), tag="l0f g")
    tri4 = k.sb("c_tri4", [128, 4 * 512], BF16)
    k.dmacast(tri4[:], W["c_tri4"].ap(), w=("c.tri4",), tag="l0f tri4")

    mark0 = k.arena_mark()
    YT = k.sb("l0f_YT", [128, NCT * T], BF16)
    Lr = k.sb("l0f_Lr", [128, NCT * T], BF16)
    emit_norm(k, ("sb", R, lambda ct: ("l0f.xln",)), Lr, "rms",
              gain_sb=gg[:, 0:NCT], gkey="l0f.g", eps=1e-6, pfx="l0f1")
    lrap = lambda ct, sl: Lr[:, ct * T + sl.start: ct * T + sl.stop]
    lrkey = lambda ct: (f"l0f1.xb{ct}",)

    QT = k.sb("l0f_QT", [128, NCT * T], BF16)
    cT = k.sb("l0f_cT", [128, 4 * T], BF16)
    wqS = WStream(k, "l0fwq", W["l0f_wq"], W["wq_meta"])
    wdkvS = WStream(k, "l0fwdkv", W["l0f_wdkv"], W["wdkv_meta"])
    wukS = WStream(k, "l0fwuk", W["l0f_wuk"], W["wuk_meta"])
    wuvS = WStream(k, "l0fwuv", W["l0f_wuv"], W["wuv_meta"])

    def qevac(mt, half, ps_list, pkeys):
        for ch in range(2):
            sl = slice(mt * T + half * 1024 + ch * 512, mt * T + half * 1024 + (ch + 1) * 512)
            k.v("tensor_copy", QT[:, sl], ps_list[ch], r=(pkeys[ch],), w=("l0f.QT",),
                tag=f"l0f qe{mt}.{half}.{ch}")
    emit_gemm(k, wqS, lrap, lrkey, qevac, "l0f_q")

    def cevac(mt, half, ps_list, pkeys):
        for ch in range(2):
            sl = slice(mt * T + half * 1024 + ch * 512, mt * T + half * 1024 + (ch + 1) * 512)
            k.v("tensor_copy", cT[:, sl], ps_list[ch], r=(pkeys[ch],), w=("l0f.cT",),
                tag=f"l0f ce{mt}.{half}.{ch}")
    emit_gemm(k, wdkvS, lrap, lrkey, cevac, "l0f_c")
    ctap = lambda ct, sl: cT[:, ct * T + sl.start: ct * T + sl.stop]
    ctkey = lambda ct: ("l0f.cT",)

    KT = k.sb("l0f_KT", [128, T], BF16)
    vtmp = k.sb("l0f_vtmp", [128, 1024], BF16)
    vaug = k.sb("l0f_vaug", [128, 2 * NTT * 65], BF16)
    ones_ap = bass.AP(vaug, 64, [[2 * NTT * 65, 128], [65, 2 * NTT]])
    k.emit("gpsimd", lambda e: e.memset(ones_ap, 1.0), w=("l0f.vaug_ones",),
           tag="l0f vaug ones")
    sqb = k.sb("l0f_sqb", [128, T], BF16)
    mrows = k.sb("l0f_mrows", [1, 16], F32)
    nbf = k.sb("l0f_nbf", [1, 2], BF16)
    negBc = k.sb("l0f_negBc", [128, 2], F32)
    abuf = k.sb("l0f_abuf", [128, 2 * 512], BF16)
    ytmp = k.sb("l0f_ytmp", [128, 512], BF16)
    recb = k.sb("l0f_recb", [1, 512], BF16)

    for p in range(NCT):
        if p == 0:
            wukS.load(0), wuvS.load(0)
        for half in range(2):
            bks = ("psA0", "psA1") if half == 0 else ("psC0", "psC1")
            for ct in range(4):
                statK, wkeyK = wukS.tile(p, ct)
                for ch in range(2):
                    sl = slice(half * 1024 + ch * 512, half * 1024 + (ch + 1) * 512)
                    k.mm(PS[bks[ch]][:], statK, ctap(ct, sl), start=(ct == 0),
                         stop=(ct == 3), r=(wkeyK,) + ctkey(ct), w=(bks[ch],),
                         tag=f"l0f kmm{p}.{half}.{ch}.{ct}")
            for ch in range(2):
                sl = slice(half * 1024 + ch * 512, half * 1024 + (ch + 1) * 512)
                k.v("tensor_copy", KT[:, sl], PS[bks[ch]][:], r=(bks[ch],),
                    w=("l0f.KT",), tag=f"l0f kev{p}.{half}.{ch}")
        for half in range(2):
            bks = ("psA0", "psA1") if half == 0 else ("psC0", "psC1")
            for ct in range(4):
                statV, wkeyV = wuvS.tile(p, ct)
                for ch in range(2):
                    sl = slice(half * 1024 + ch * 512, half * 1024 + (ch + 1) * 512)
                    k.mm(PS[bks[ch]][:], statV, ctap(ct, sl), start=(ct == 0),
                         stop=(ct == 3), r=(wkeyV,) + ctkey(ct), w=(bks[ch],),
                         tag=f"l0f vmm{p}.{half}.{ch}.{ct}")
            for ch in range(2):
                k.v("tensor_copy", vtmp[:, ch * 512:(ch + 1) * 512], PS[bks[ch]][:],
                    r=(bks[ch],), w=("l0f.vtmp",), tag=f"l0f vev{p}.{half}.{ch}")
            for h2 in range(2):
                for tt in range(8):
                    tpar = tt % 2
                    pst = PS[f"psT{tpar}"]
                    k.trans(pst[:, 0:64], vtmp[h2 * 64:(h2 + 1) * 64, tt * 128:(tt + 1) * 128],
                            k.ident(h2 * 64, 64), r=("l0f.vtmp",), w=(f"psT{tpar}",),
                            tag=f"l0f vt{p}.{half}.{h2}.{tt}")
                    kt = half * 8 + tt
                    k.v("tensor_copy",
                        vaug[:, h2 * NTT * 65 + kt * 65: h2 * NTT * 65 + kt * 65 + 64],
                        pst[:, 0:64], r=(f"psT{tpar}",), w=(f"l0f.vaug{h2}.{kt}",),
                        tag=f"l0f vc{p}.{half}.{h2}.{tt}")
        if p + 1 < NCT:
            wukS.load(p + 1), wuvS.load(p + 1)

        for h2 in range(2):
            hsl = slice(h2 * 64, (h2 + 1) * 64)
            nbck = f"l0f.negBc{h2}"
            for which, buf, key in (("q", QT, "l0f.QT"), ("K", KT, "l0f.KT")):
                src = buf[hsl, p * T:(p + 1) * T] if which == "q" else buf[hsl, :]
                k.v("tensor_mul", sqb[hsl, :], src, src, r=(key,),
                    w=("l0f.sqb",), tag=f"l0f sq{which}{p}.{h2}")
                for ch in range(NQC):
                    k.mm(PS["psB1"][0:1, 0:512], k.ones_col[hsl, :],
                         sqb[hsl, ch * 512:(ch + 1) * 512], True, True,
                         r=("l0f.sqb", "c.ones_col"), w=("psB1",),
                         tag=f"l0f nbm{which}{p}.{h2}.{ch}")
                    k.v("reduce_max", mrows[0:1, 4 + ch:5 + ch], PS["psB1"][0:1, 0:512],
                        axis=AX, r=("psB1",), w=("l0f.mch",),
                        tag=f"l0f nbx{which}{p}.{h2}.{ch}")
                k.v("reduce_max", mrows[0:1, (1 if which == "q" else 2):
                                        (2 if which == "q" else 3)],
                    mrows[0:1, 4:4 + NQC], axis=AX, r=("l0f.mch",),
                    w=(f"l0f.m{which}",), tag=f"l0f nbr{which}{p}.{h2}")
            k.v("tensor_add", mrows[0:1, 0:1], mrows[0:1, 1:2], mrows[0:1, 2:3],
                r=("l0f.mq", "l0f.mK"), w=("l0f.b0",), tag=f"l0f b0 {p}.{h2}")
            k.v("tensor_scalar_mul", mrows[0:1, 3:4], mrows[0:1, 0:1], -SC * 0.5,
                r=("l0f.b0",), w=("l0f.b2",), tag=f"l0f b2 {p}.{h2}")
            k.v("tensor_copy", nbf[0:1, h2:h2 + 1], mrows[0:1, 3:4], r=("l0f.b2",),
                w=("l0f.nbf",), tag=f"l0f nbf {p}.{h2}")
            k.mm(PS["psB1"][:, 0:1], k.ones_row[:], nbf[0:1, h2:h2 + 1], True, True,
                 r=("l0f.nbf", "c.ones_row"), w=("psB1",), tag=f"l0f nbc{p}.{h2}")
            k.v("tensor_copy", negBc[:, h2:h2 + 1], PS["psB1"][:, 0:1], r=("psB1",),
                w=(nbck,), tag=f"l0f nbe{p}.{h2}")

            for qc in range(NQC):
                last_kt = 4 * qc + 3
                for kt in range(last_kt + 1):
                    slot = kt % 2
                    psS = PS["psA" + str(slot)]
                    abk = f"l0f.ab{slot}"
                    av = abuf[:, slot * 512:(slot + 1) * 512]
                    k.mm(psS[:], KT[hsl, kt * 128:(kt + 1) * 128],
                         QT[hsl, p * T + qc * 512:p * T + (qc + 1) * 512], True, True,
                         r=("l0f.KT", "l0f.QT"), w=(f"psA{slot}",),
                         tag=f"l0f S{p}.{h2}.{qc}.{kt}")
                    k.act(av, psS[:], AF.Exp, bias=negBc[:, h2:h2 + 1],
                          scale=k.cc(SC), r=(f"psA{slot}", nbck), w=(abk,),
                          tag=f"l0f E{p}.{h2}.{qc}.{kt}")
                    g = kt - 4 * qc
                    if 0 <= g <= 3:
                        k.v("tensor_mul", av, av, tri4[:, g * 512:(g + 1) * 512],
                            r=(abk, "c.tri4"), w=(abk,), tag=f"l0f M{p}.{h2}.{qc}.{kt}")
                    k.mm(PS["psB0"][0:65, :],
                         vaug[:, h2 * NTT * 65 + kt * 65: h2 * NTT * 65 + (kt + 1) * 65],
                         av, start=(kt == 0), stop=(kt == last_kt),
                         r=(abk, f"l0f.vaug{h2}.{kt}", "l0f.vaug_ones"),
                         w=("psB0",), tag=f"l0f av{p}.{h2}.{qc}.{kt}")
                k.v("reciprocal", recb[0:1, :], PS["psB0"][64:65, :], r=("psB0",),
                    w=("l0f.recb",), tag=f"l0f rc{p}.{h2}.{qc}", lowp=True)
                k.mm(PS["psB1"][0:64, :], k.ones_row[0:1, 0:64], recb[0:1, :],
                     True, True, r=("l0f.recb", "c.ones_row"), w=("psB1",),
                     tag=f"l0f rb{p}.{h2}.{qc}")
                k.act(ytmp[0:64, :], PS["psB0"][0:64, :], AF.Copy, r=("psB0",),
                      w=("l0f.ytmp",), tag=f"l0f yc{p}.{h2}.{qc}")
                if h2 == 0:
                    k.v("tensor_mul", YT[0:64, p * T + qc * 512:p * T + (qc + 1) * 512],
                        ytmp[0:64, :], PS["psB1"][0:64, :], r=("l0f.ytmp", "psB1"),
                        w=("l0f.yt",), tag=f"l0f ym{p}.{h2}.{qc}")
                else:
                    k.v("tensor_mul", ytmp[0:64, :], ytmp[0:64, :], PS["psB1"][0:64, :],
                        r=("l0f.ytmp", "psB1"), w=("l0f.ytmp",), tag=f"l0f ym{p}.{h2}.{qc}")
                    k.dmac(YT[64:128, p * T + qc * 512:p * T + (qc + 1) * 512],
                           ytmp[0:64, :], r=("l0f.ytmp",), w=("l0f.yt",),
                           tag=f"l0f ymv{p}.{h2}.{qc}")

    woS = WStream(k, "l0fwo", W["l0f_wo"], W["wo_meta"])

    def roevac(mt, half, ps_list, pkeys):
        for ch in range(2):
            sl = slice(mt * T + half * 1024 + ch * 512, mt * T + half * 1024 + (ch + 1) * 512)
            k.v("tensor_add", R[:, sl], R[:, sl], ps_list[ch],
                r=("l0f.xln", pkeys[ch]), w=("l0f.xln",), tag=f"l0f ro{mt}.{half}.{ch}")
    emit_gemm(k, woS, lambda ct, sl: YT[:, ct * T + sl.start:ct * T + sl.stop],
              lambda ct: ("l0f.yt",), roevac, "l0f_o")

    k.arena_release(mark0)
    L2r = k.sb("l0f_L2r", [128, NCT * T], BF16)
    emit_norm(k, ("sb", R, lambda ct: ("l0f.xln",)), L2r, "rms",
              gain_sb=gg[:, NCT:2 * NCT], gkey="l0f.g", eps=1e-6, pfx="l0f2")
    l2ap = lambda ct, sl: L2r[:, ct * T + sl.start: ct * T + sl.stop]
    l2key = lambda ct: (f"l0f2.xb{ct}",)

    gateS = WStream(k, "l0fgate", W["l0f_gate"], W["gate_meta"])
    logT = k.sb("l0f_logT", [8, T], BF16)

    def gevac(mt, half, ps_list, pkeys):
        for ch in range(2):
            sl = slice(half * 1024 + ch * 512, half * 1024 + (ch + 1) * 512)
            k.v("tensor_copy", logT[:, sl], ps_list[ch], r=(pkeys[ch],),
                w=("l0f.logT",), tag=f"l0f ge{half}.{ch}")
    emit_gemm(k, gateS, l2ap, l2key, gevac, "l0f_g8")
    ltok = k.sb("l0f_ltok", [128, NTT * 8], F32)
    wtok = k.sb("l0f_wtok", [128, NTT * 8], BF16)
    rt1 = k.sb("l0f_rt1", [128, 4 * 8], F32)
    rcol = k.sb("l0f_rcol", [128, 4], F32)
    for tt in range(NTT):
        tpar = tt % 2
        pst = PS[f"psT{tpar}"]
        k.trans(pst[0:128, 0:8], logT[0:8, tt * 128:(tt + 1) * 128], k.ident(0, 8),
                r=("l0f.logT",), w=(f"psT{tpar}",), tag=f"l0f lt{tt}")
        k.v("tensor_copy", ltok[:, tt * 8:(tt + 1) * 8], pst[0:128, 0:8],
            r=(f"psT{tpar}",), w=("l0f.ltok",), tag=f"l0f lc{tt}")
        lv = ltok[:, tt * 8:(tt + 1) * 8]
        k.v("reduce_max", rcol[:, 0:1], lv, axis=AX, r=("l0f.ltok",), w=("l0f.m1",),
            tag=f"l0f m1{tt}")
        k.v("tensor_single_scalar", rt1[:, 0:8], lv, rcol[:, 0:1], ALU.is_equal,
            r=("l0f.ltok", "l0f.m1"), w=("l0f.eq1",), tag=f"l0f eq1{tt}")
        k.v("scalar_tensor_tensor", rt1[:, 8:16], rt1[:, 0:8], 1e9, lv,
            ALU.mult, ALU.subtract, r=("l0f.eq1", "l0f.ltok"), w=("l0f.msk",),
            tag=f"l0f msk{tt}")
        k.v("tensor_scalar_mul", rt1[:, 8:16], rt1[:, 8:16], -1.0,
            r=("l0f.msk",), w=("l0f.msk",), tag=f"l0f mskn{tt}")
        k.v("reduce_max", rcol[:, 1:2], rt1[:, 8:16], axis=AX, r=("l0f.msk",),
            w=("l0f.m2",), tag=f"l0f m2{tt}")
        k.v("tensor_single_scalar", rt1[:, 16:24], rt1[:, 8:16], rcol[:, 1:2],
            ALU.is_equal, r=("l0f.msk", "l0f.m2"), w=("l0f.eq2",), tag=f"l0f eq2{tt}")
        k.v("tensor_sub", rcol[:, 2:3], rcol[:, 0:1], rcol[:, 1:2],
            r=("l0f.m1", "l0f.m2"), w=("l0f.d12",), tag=f"l0f d12{tt}")
        k.act(rcol[:, 3:4], rcol[:, 2:3], AF.Sigmoid, r=("l0f.d12",), w=("l0f.g1",),
              tag=f"l0f g1{tt}")
        k.v("tensor_sub", rt1[:, 24:32], rt1[:, 0:8], rt1[:, 16:24],
            r=("l0f.eq1", "l0f.eq2"), w=("l0f.dq",), tag=f"l0f dq{tt}")
        k.v("scalar_tensor_tensor", wtok[:, tt * 8:(tt + 1) * 8], rt1[:, 24:32],
            rcol[:, 3:4], rt1[:, 16:24], ALU.mult, ALU.add,
            r=("l0f.dq", "l0f.g1", "l0f.eq2"), w=("l0f.wtok",), tag=f"l0f wt{tt}")
    wT = k.sb("l0f_wT", [8, T], BF16)
    for tt in range(NTT):
        tpar = tt % 2
        pst = PS[f"psT{tpar}"]
        k.trans(pst[0:8, 0:128], wtok[:, tt * 8:(tt + 1) * 8], k.ident(0, 128),
                r=("l0f.wtok",), w=(f"psT{tpar}",), tag=f"l0f wtt{tt}")
        k.v("tensor_copy", wT[0:8, tt * 128:(tt + 1) * 128], pst[0:8, 0:128],
            r=(f"psT{tpar}",), w=("l0f.wT",), tag=f"l0f wtc{tt}")

    w1S = WStream(k, "l0fw1", W["l0f_w1"], W["w1_meta"])
    w2S = WStream(k, "l0fw2", W["l0f_w2"], W["w2_meta"])
    w1_order = [e * 16 + m for tc in range(2) for e in range(8) for m in range(16)]
    w2_order = [e * 8 + m for tc in range(2) for e in range(8) for m in range(8)]
    w1_cur = [0]
    w2_cur = [0]

    def w1_use(idx):
        while w1_cur[0] <= min(idx + 1, len(w1_order) - 1):
            w1S.load(w1_order[w1_cur[0]], b=w1_cur[0] % 2)
            w1_cur[0] += 1

    def w2_use(idx):
        while w2_cur[0] <= min(idx + 1, len(w2_order) - 1):
            w2S.load(w2_order[w2_cur[0]], b=w2_cur[0] % 2)
            w2_cur[0] += 1
    hid = k.sb("l0f_hid", [128, 16 * 1024], BF16)
    wrow = k.sb("l0f_wrow", [1, 1024], BF16)
    acc = k.sb("l0f_acc", [128, 8 * 1024], F32)
    wbc = k.sb("l0f_wbc", [128, 1024], BF16)
    xfp = k.sb("l0f_xfp", [128, 1024], F32)
    xo = k.sb("l0f_xo", [128, 1024], F32)
    for tc in range(2):
        tsl = slice(tc * 1024, (tc + 1) * 1024)
        for e in range(8):
            k.dmac(wrow[:], wT[e:e + 1, tc * 1024:(tc + 1) * 1024],
                   r=("l0f.wT",), w=("l0f.wrow",), tag=f"l0f wr{tc}.{e}")
            for ch in range(2):
                k.mm(PS["psC0"][:], k.ones_row[:],
                     wrow[0:1, ch * 512:(ch + 1) * 512],
                     True, True, r=("l0f.wrow", "c.ones_row"), w=("psC0",),
                     tag=f"l0f wb{tc}.{e}.{ch}")
                k.v("tensor_copy", wbc[:, ch * 512:(ch + 1) * 512], PS["psC0"][:],
                    r=("psC0",), w=("l0f.wbc",), tag=f"l0f wbe{tc}.{e}.{ch}")
            for mtt in range(16):
                mt = e * 16 + mtt
                i1 = tc * 128 + e * 16 + mtt
                w1_use(i1)
                for ct in range(NCT):
                    stat, wkey = w1S.tile(mt, ct, b=i1 % 2)
                    for ch in range(2):
                        bk = ("psA0", "psA1")[ch]
                        sl = slice(tc * 1024 + ch * 512, tc * 1024 + (ch + 1) * 512)
                        k.mm(PS[bk][:], stat, l2ap(ct, sl), start=(ct == 0),
                             stop=(ct == NCT - 1), r=(wkey,) + l2key(ct), w=(bk,),
                             tag=f"l0f h{tc}.{e}.{mtt}.{ch}")
                for ch in range(2):
                    bk = ("psA0", "psA1")[ch]
                    k.act(hid[:, mtt * 1024 + ch * 512: mtt * 1024 + (ch + 1) * 512],
                          PS[bk][:], AF.Silu, r=(bk,), w=("l0f.hid",),
                          tag=f"l0f he{tc}.{e}.{mtt}.{ch}")
            for mtd in range(8):
                mt = e * 8 + mtd
                i2 = tc * 64 + e * 8 + mtd
                w2_use(i2)
                for hct in range(16):
                    stat, wkey = w2S.tile(mt, hct, b=i2 % 2)
                    for ch in range(2):
                        bk = ("psB0", "psB1")[ch]
                        k.mm(PS[bk][:], stat,
                             hid[:, hct * 1024 + ch * 512: hct * 1024 + (ch + 1) * 512],
                             start=(hct == 0), stop=(hct == 15),
                             r=(wkey, "l0f.hid"), w=(bk,),
                             tag=f"l0f o{tc}.{e}.{mtd}.{ch}")
                for ch in range(2):
                    bk = ("psB0", "psB1")[ch]
                    dst = acc[:, mtd * 1024 + ch * 512: mtd * 1024 + (ch + 1) * 512]
                    if e == 0:
                        k.v("tensor_mul", dst, PS[bk][:], wbc[:, ch * 512:(ch + 1) * 512],
                            r=(bk, "l0f.wbc"), w=("l0f.acc",),
                            tag=f"l0f ac{tc}.{e}.{mtd}.{ch}")
                    else:
                        k.v("tensor_mul", xo[:, ch * 512:(ch + 1) * 512], PS[bk][:],
                            wbc[:, ch * 512:(ch + 1) * 512],
                            r=(bk, "l0f.wbc"), w=("l0f.t",),
                            tag=f"l0f am{tc}.{e}.{mtd}.{ch}")
                        k.v("tensor_add", dst, dst, xo[:, ch * 512:(ch + 1) * 512],
                            r=("l0f.acc", "l0f.t"), w=("l0f.acc",),
                            tag=f"l0f aa{tc}.{e}.{mtd}.{ch}")
        for mt in range(8):
            k.dmac(xfp[:], x_in.ap()[mt * 128:(mt + 1) * 128, tsl], w=("l0f.xfp",),
                   tag=f"l0f xl{tc}.{mt}")
            k.v("tensor_add", xo[:], xfp[:], R[:, mt * T + tc * 1024: mt * T + (tc + 1) * 1024],
                r=("l0f.xfp", "l0f.xln"), w=("l0f.t",), tag=f"l0f fa{tc}.{mt}")
            k.v("tensor_add", xo[:], xo[:], acc[:, mt * 1024:(mt + 1) * 1024],
                r=("l0f.t", "l0f.acc"), w=("l0f.t",), tag=f"l0f fb{tc}.{mt}")
            k.dmac(x_out.ap()[mt * 128:(mt + 1) * 128, tsl], xo[:],
                   r=("l0f.t",), w=(f"xout.l0f.{mt}.{tc}",), tag=f"l0f xs{tc}.{mt}")
    return [f"xout.l0f.{mt}.{tc}" for mt in range(8) for tc in range(2)]




def prep_l2_attn(inputs):
    arrs, meta = {}, {}
    for nm, w in (("wq", inputs["L2_attn_Wq"]), ("wk", inputs["L2_attn_Wk"]),
                  ("wv", inputs["L2_attn_Wv"]), ("cg", inputs["L2_attn_cg_W"]),
                  ("wca", inputs["L2_attn_ca_Win"]), ("wcao", inputs["L2_attn_ca_Wout"]),
                  ("wmc", inputs["L2_attn_mc_Win"]), ("wmco", inputs["L2_attn_mc_Wout"]),
                  ("wo", inputs["L2_attn_Wo"])):
        arr, m = host_tiles(np.asarray(w, np.float32))
        arrs["l2a_" + nm] = arr
        meta[nm + "_meta"] = m
    arr, m = host_tiles(np.asarray(inputs["L2_attn_am_W"], np.float32))
    arrs["l2a_am"] = arr
    meta["am_meta"] = m
    def pack(v, n):
        return np.ascontiguousarray(np.asarray(v, np.float32).reshape(n, 128).T)
    arrs["l2a_bq"] = pack(inputs["L2_attn_bq"], 8)
    arrs["l2a_bk"] = pack(inputs["L2_attn_bk"], 8)
    arrs["l2a_bv"] = pack(inputs["L2_attn_bv"], 8)
    arrs["l2a_bca"] = pack(inputs["L2_attn_ca_bin"], 24)
    arrs["l2a_bcao"] = pack(inputs["L2_attn_ca_bout"], 8)
    arrs["l2a_bmc"] = pack(inputs["L2_attn_mc_bin"], 24)
    arrs["l2a_bmco"] = pack(inputs["L2_attn_mc_bout"], 8)
    arrs["l2a_bo"] = pack(inputs["L2_attn_bo"], 8)
    arrs["l2a_cons"] = np.asarray(inputs["L2_attn_cons_vec"], np.float32).reshape(16, 1)
    arrs["l2a_cgb"] = np.asarray(inputs["L2_attn_cg_b"], np.float32).reshape(16, 1)
    arrs["l2a_amb"] = np.asarray(inputs["L2_attn_am_b"], np.float32).reshape(16, 1)
    return arrs, meta


def _attn64(k, pfx, QT, KT, qkeys, vaug, vaug_keyf, YT, yt_key, negBc, nbkeys, SC,
            col0=0, scr=None):
    T, PS = k.T, k.PS
    NTT, NQC = T // 128, T // 512
    if scr is None:
        scr = {}
    if "abuf" not in scr:
        scr["abuf"] = k.sb(pfx + "_abuf", [128, 2 * 512], BF16)
        scr["ytmp"] = k.sb(pfx + "_ytmp", [128, 512], BF16)
        scr["recb"] = k.sb(pfx + "_recb", [1, 512], BF16)
    abuf, ytmp, recb = scr["abuf"], scr["ytmp"], scr["recb"]
    for h2 in range(2):
        hsl = slice(h2 * 64, (h2 + 1) * 64)
        for qc in range(NQC):
            last_kt = NTT - 1
            for kt in range(last_kt + 1):
                slot = kt % 2
                psS = PS["psA" + str(slot)]
                abk = f"{pfx}.ab{slot}"
                av = abuf[:, slot * 512:(slot + 1) * 512]
                k.mm(psS[:], KT[hsl, kt * 128:(kt + 1) * 128],
                     QT[hsl, qc * 512:(qc + 1) * 512], True, True,
                     r=qkeys, w=(f"psA{slot}",), tag=f"{pfx} S{h2}.{qc}.{kt}")
                k.act(av, psS[:], AF.Exp, bias=negBc[:, h2:h2 + 1], scale=k.cc(SC),
                      r=(f"psA{slot}",) + nbkeys, w=(abk,), tag=f"{pfx} E{h2}.{qc}.{kt}")
                k.mm(PS["psB0"][0:65, :],
                     vaug[:, h2 * NTT * 65 + kt * 65: h2 * NTT * 65 + (kt + 1) * 65],
                     av, start=(kt == 0), stop=(kt == last_kt),
                     r=(abk,) + vaug_keyf(h2, kt), w=("psB0",),
                     tag=f"{pfx} av{h2}.{qc}.{kt}")
            k.v("reciprocal", recb[0:1, :], PS["psB0"][64:65, :], r=("psB0",),
                w=(f"{pfx}.recb",), tag=f"{pfx} rc{h2}.{qc}", lowp=True)
            k.mm(PS["psB1"][0:64, :], k.ones_row[0:1, 0:64], recb[0:1, :], True, True,
                 r=(f"{pfx}.recb", "c.ones_row"), w=("psB1",), tag=f"{pfx} rb{h2}.{qc}")
            k.act(ytmp[0:64, :], PS["psB0"][0:64, :], AF.Copy, r=("psB0",),
                  w=(f"{pfx}.ytmp",), tag=f"{pfx} yc{h2}.{qc}")
            if h2 == 0:
                k.v("tensor_mul", YT[0:64, col0 + qc * 512:col0 + (qc + 1) * 512],
                    ytmp[0:64, :], PS["psB1"][0:64, :],
                    r=(f"{pfx}.ytmp", "psB1"), w=(yt_key,), tag=f"{pfx} ym{h2}.{qc}")
            else:
                k.v("tensor_mul", ytmp[0:64, :], ytmp[0:64, :], PS["psB1"][0:64, :],
                    r=(f"{pfx}.ytmp", "psB1"), w=(f"{pfx}.ytmp",), tag=f"{pfx} ym{h2}.{qc}")
                k.dmac(YT[64:128, col0 + qc * 512:col0 + (qc + 1) * 512], ytmp[0:64, :],
                       r=(f"{pfx}.ytmp",), w=(yt_key,), tag=f"{pfx} ymv{h2}.{qc}")


def _neg_bound(k, pfx, bufs, mrows, nbf, negBc, h2, SC, sqb):
    T, PS = k.T, k.PS
    NQC = T // 512
    col = 1
    for (ap_, key_, base, cnt) in bufs:
        k.v("tensor_mul", sqb[base:base + cnt, :], ap_, ap_, r=(key_,),
            w=(f"{pfx}.sqb",), tag=f"{pfx} nbs{h2}.{col}")
        for ch in range(NQC):
            k.mm(PS["psB1"][0:1, 0:512], k.ones_col[base:base + cnt, :],
                 sqb[base:base + cnt, ch * 512:(ch + 1) * 512], True, True,
                 r=(f"{pfx}.sqb", "c.ones_col"), w=("psB1",),
                 tag=f"{pfx} nbm{h2}.{col}.{ch}")
            k.v("reduce_max", mrows[0:1, 4 + ch:5 + ch], PS["psB1"][0:1, 0:512],
                axis=AX, r=("psB1",), w=(f"{pfx}.mch",), tag=f"{pfx} nbx{h2}.{col}.{ch}")
        k.v("reduce_max", mrows[0:1, col:col + 1], mrows[0:1, 4:4 + NQC], axis=AX,
            r=(f"{pfx}.mch",), w=(f"{pfx}.m{col}",), tag=f"{pfx} nbr{h2}.{col}")
        col += 1
    mk_ = tuple(f"{pfx}.m{i}" for i in range(1, col))
    k.v("tensor_add", mrows[0:1, 0:1], mrows[0:1, 1:2], mrows[0:1, 2:3],
        r=mk_[:2], w=(f"{pfx}.b0",), tag=f"{pfx} b0.{h2}")
    for i in range(3, col):
        k.v("tensor_add", mrows[0:1, 0:1], mrows[0:1, 0:1], mrows[0:1, i:i + 1],
            r=(f"{pfx}.b0", mk_[i - 1]), w=(f"{pfx}.b0",), tag=f"{pfx} b0{i}.{h2}")
    k.v("tensor_scalar_mul", mrows[0:1, 3:4], mrows[0:1, 0:1], -SC * 0.5,
        r=(f"{pfx}.b0",), w=(f"{pfx}.b2",), tag=f"{pfx} b2.{h2}")
    k.v("tensor_copy", nbf[0:1, h2:h2 + 1], mrows[0:1, 3:4], r=(f"{pfx}.b2",),
        w=(f"{pfx}.nbf",), tag=f"{pfx} nbf.{h2}")
    k.mm(PS["psB1"][:, 0:1], k.ones_row[:], nbf[0:1, h2:h2 + 1], True, True,
         r=(f"{pfx}.nbf", "c.ones_row"), w=("psB1",), tag=f"{pfx} nbc.{h2}")
    k.v("tensor_copy", negBc[:, h2:h2 + 1], PS["psB1"][:, 0:1], r=("psB1",),
        w=(f"{pfx}.negBc{h2}",), tag=f"{pfx} nbe.{h2}")


def emit_l2_attn(k, x_in, x_out, W):
    T, PS = k.T, k.PS
    NTT, NQC = T // 128, T // 512
    xln = k.sb("l2a_xln", [128, NCT * T], BF16)
    emit_norm(k, x_in, xln, "ln", pfx="l2a")
    xap = lambda ct, sl: xln[:, ct * T + sl.start: ct * T + sl.stop]
    xkey = lambda ct: (f"l2a.xb{ct}",)

    vecs = {}
    for nm, nc_ in (("bq", 8), ("bk", 8), ("bv", 8), ("bca", 24), ("bcao", 8),
                    ("bmc", 24), ("bmco", 8), ("bo", 8)):
        vecs[nm] = k.sb("l2a_" + nm, [128, nc_], F32)
        k.dmac(vecs[nm][:], W["l2a_" + nm].ap(), w=(f"l2a.{nm}",), tag=f"l2a {nm}")
    cons = k.sb("l2a_cons", [16, 1], BF16)
    k.dmacast(cons[:], W["l2a_cons"].ap(), w=("l2a.cons",), tag="l2a cons")
    cgb = k.sb("l2a_cgb", [16, 1], F32)
    k.dmac(cgb[:], W["l2a_cgb"].ap(), w=("l2a.cgb",), tag="l2a cgb")
    amb = k.sb("l2a_amb", [16, 1], F32)
    k.dmac(amb[:], W["l2a_amb"].ap(), w=("l2a.amb",), tag="l2a amb")

    amS = WStream(k, "l2aam", W["l2a_am"], W["am_meta"])
    amS.load(0)
    modb = k.sb("l2a_modb", [16, 1], F32)
    stat, wkey = amS.tile(0, 0)
    k.mm(PS["psC0"][0:16, 0:1], stat[0:16, :], cons[:], True, True,
         r=(wkey, "l2a.cons"), w=("psC0",), tag="l2a am")
    k.v("tensor_add", modb[:], PS["psC0"][0:16, 0:1], amb[:],
        r=("psC0", "l2a.amb"), w=("l2a.modb",), tag="l2a modb1")
    k.v("tensor_add", modb[:], modb[:], cgb[:], r=("l2a.modb", "l2a.cgb"),
        w=("l2a.modb",), tag="l2a modb2")
    modT = k.sb("l2a_modT", [16, T], F32)
    cgS = WStream(k, "l2acg", W["l2a_cg"], W["cg_meta"])

    def modevac(mt, half, ps_list, pkeys):
        for ch in range(2):
            sl = slice(half * 1024 + ch * 512, half * 1024 + (ch + 1) * 512)
            k.act(modT[0:16, sl], ps_list[ch], AF.Sigmoid, bias=modb[:],
                  r=(pkeys[ch], "l2a.modb"), w=("l2a.modT",), tag=f"l2a mod{half}{ch}")
    emit_gemm(k, cgS, xap, xkey, modevac, "l2a_mod")
    modrow = k.sb("l2a_modrow", [1, T], BF16)

    YTm = k.sb("l2a_YTm", [128, NCT * T], BF16)
    sqb = k.sb("l2a_sqb", [128, T], BF16)
    mrows = k.sb("l2a_mrows", [1, 16], F32)
    nbf = k.sb("l2a_nbf", [1, 2], BF16)
    negBc = k.sb("l2a_negBc", [128, 2], F32)

    mark1 = k.arena_mark()
    wqS = WStream(k, "l2awq", W["l2a_wq"], W["wq_meta"])
    wkS = WStream(k, "l2awk", W["l2a_wk"], W["wk_meta"])
    wvS = WStream(k, "l2awv", W["l2a_wv"], W["wv_meta"])
    QT = k.sb("l2a_QT", [128, T], BF16)
    KT = k.sb("l2a_KT", [128, T], BF16)
    modbc = k.sb("l2a_modbc", [128, T], BF16)
    vtmp = k.sb("l2a_vtmp", [128, 1024], BF16)
    vaug = k.sb("l2a_vaug", [128, 2 * NTT * 65], BF16)
    ones_ap = bass.AP(vaug, 64, [[2 * NTT * 65, 128], [65, 2 * NTT]])
    k.emit("gpsimd", lambda e: e.memset(ones_ap, 1.0), w=("l2a.vaug_ones",),
           tag="l2a vaug ones")

    a64scr = {}
    for p in range(NCT):
        if p == 0:
            wqS.load(0), wkS.load(0), wvS.load(0)
        for h2 in range(2):
            h = 2 * p + h2
            k.dmacast(modrow[0:1, :], modT[h:h + 1, :], r=("l2a.modT",),
                      w=("l2a.modrow",), tag=f"l2a mr{h}")
            for ch in range(NQC):
                bk = ("psC0", "psC1")[ch % 2]
                tp = (0, 64) if h2 == 1 else None
                psl = PS[bk][64:128, :] if h2 == 1 else PS[bk][0:64, :]
                k.mm(psl, k.ones_row[0:1, 0:64],
                     modrow[0:1, ch * 512:(ch + 1) * 512], True, True,
                     r=("l2a.modrow", "c.ones_row"), w=(bk,), tile_position=tp,
                     tag=f"l2a mb{p}.{h2}.{ch}")
                k.v("tensor_copy", modbc[h2 * 64:(h2 + 1) * 64, ch * 512:(ch + 1) * 512],
                    psl, r=(bk,), w=("l2a.modbc",), tag=f"l2a mbe{p}.{h2}.{ch}")
        for half in range(2):
            bks = ("psA0", "psA1") if half == 0 else ("psC0", "psC1")
            for ct in range(NCT):
                statq, wkeyq = wqS.tile(p, ct)
                for ch in range(2):
                    sl = slice(half * 1024 + ch * 512, half * 1024 + (ch + 1) * 512)
                    k.mm(PS[bks[ch]][:], statq, xap(ct, sl), start=(ct == 0),
                         stop=(ct == NCT - 1), r=(wkeyq,) + xkey(ct), w=(bks[ch],),
                         tag=f"l2a qm{p}.{half}.{ch}.{ct}")
            for ch in range(2):
                sl = slice(half * 1024 + ch * 512, half * 1024 + (ch + 1) * 512)
                k.v("scalar_tensor_tensor", QT[:, sl], PS[bks[ch]][:],
                    vecs["bq"][:, p:p + 1], modbc[:, sl], ALU.add, ALU.mult,
                    r=(bks[ch], "l2a.bq", "l2a.modbc"), w=("l2a.QT",),
                    tag=f"l2a qe{p}.{half}.{ch}")
        for half in range(2):
            bks = ("psA0", "psA1") if half == 0 else ("psC0", "psC1")
            for ct in range(NCT):
                statk, wkeyk = wkS.tile(p, ct)
                for ch in range(2):
                    sl = slice(half * 1024 + ch * 512, half * 1024 + (ch + 1) * 512)
                    k.mm(PS[bks[ch]][:], statk, xap(ct, sl), start=(ct == 0),
                         stop=(ct == NCT - 1), r=(wkeyk,) + xkey(ct), w=(bks[ch],),
                         tag=f"l2a km{p}.{half}.{ch}.{ct}")
            for ch in range(2):
                sl = slice(half * 1024 + ch * 512, half * 1024 + (ch + 1) * 512)
                k.v("tensor_scalar_add", KT[:, sl], PS[bks[ch]][:],
                    vecs["bk"][:, p:p + 1], r=(bks[ch], "l2a.bk"), w=("l2a.KT",),
                    tag=f"l2a ke{p}.{half}.{ch}")
        for half in range(2):
            bks = ("psA0", "psA1") if half == 0 else ("psC0", "psC1")
            for ct in range(NCT):
                statv, wkeyv = wvS.tile(p, ct)
                for ch in range(2):
                    sl = slice(half * 1024 + ch * 512, half * 1024 + (ch + 1) * 512)
                    k.mm(PS[bks[ch]][:], statv, xap(ct, sl), start=(ct == 0),
                         stop=(ct == NCT - 1), r=(wkeyv,) + xkey(ct), w=(bks[ch],),
                         tag=f"l2a vm{p}.{half}.{ch}.{ct}")
            for ch in range(2):
                k.v("tensor_scalar_add", vtmp[:, ch * 512:(ch + 1) * 512],
                    PS[bks[ch]][:], vecs["bv"][:, p:p + 1],
                    r=(bks[ch], "l2a.bv"), w=("l2a.vtmp",), tag=f"l2a ve{p}.{half}.{ch}")
            for h2 in range(2):
                for tt in range(8):
                    tpar = tt % 2
                    pst = PS[f"psT{tpar}"]
                    k.trans(pst[:, 0:64], vtmp[h2 * 64:(h2 + 1) * 64, tt * 128:(tt + 1) * 128],
                            k.ident(h2 * 64, 64), r=("l2a.vtmp",), w=(f"psT{tpar}",),
                            tag=f"l2a vt{p}.{half}.{h2}.{tt}")
                    kt = half * 8 + tt
                    k.v("tensor_copy",
                        vaug[:, h2 * NTT * 65 + kt * 65: h2 * NTT * 65 + kt * 65 + 64],
                        pst[:, 0:64], r=(f"psT{tpar}",), w=(f"l2a.vaug{h2}.{kt}",),
                        tag=f"l2a vc{p}.{half}.{h2}.{tt}")
        if p + 1 < NCT:
            wqS.load(p + 1), wkS.load(p + 1), wvS.load(p + 1)
        for h2 in range(2):
            _neg_bound(k, "l2a", [(QT[h2 * 64:(h2 + 1) * 64, :], "l2a.QT", h2 * 64, 64),
                                  (KT[h2 * 64:(h2 + 1) * 64, :], "l2a.KT", h2 * 64, 64)],
                       mrows, nbf, negBc, h2, 0.125, sqb)
        _attn64(k, "l2am", QT, KT, ("l2a.QT", "l2a.KT"), vaug,
                lambda h2, kt: (f"l2a.vaug{h2}.{kt}", "l2a.vaug_ones"),
                YTm, "l2a.ytm", negBc,
                ("l2a.negBc0", "l2a.negBc1"), 0.125, col0=p * T, scr=a64scr)
    k.arena_release(mark1)
    return emit_l2_attn_tail(k, W, xln, xap, xkey, YTm, vecs, mrows, nbf, sqb,
                             x_in, x_out)


def _mha256(k, pfx, src_ap, src_keyf, winS, bin_sb, bin_key, YT, yt_key,
            mrows, nbf, sqb):
    T, PS = k.T, k.PS
    NTT, NQC = T // 128, T // 512
    QT2 = k.sb(pfx + "_QT2", [128, 2 * T], BF16)
    KT2 = k.sb(pfx + "_KT2", [128, 2 * T], BF16)
    vtmp = k.sb(pfx + "_vtmp", [128, 1024], BF16)
    Vtok = k.sb(pfx + "_Vtok", [128, NTT * 256], BF16)
    abuf = k.sb(pfx + "_abuf", [128, 2 * 512], BF16)
    recb = k.sb(pfx + "_recb", [1, 512], BF16)
    recbc = k.sb(pfx + "_recbc", [128, 512], BF16)
    negBc = k.sb(pfx + "_negBc", [128, 1], F32)
    SC = 0.0625

    def proj(mt, b, evac):
        for half in range(2):
            bks = ("psA0", "psA1") if half == 0 else ("psC0", "psC1")
            for ct in range(NCT):
                stat, wkey = winS.tile(mt, ct, b=b)
                for ch in range(2):
                    sl = slice(half * 1024 + ch * 512, half * 1024 + (ch + 1) * 512)
                    k.mm(PS[bks[ch]][:], stat, src_ap(ct, sl), start=(ct == 0),
                         stop=(ct == NCT - 1), r=(wkey,) + src_keyf(ct), w=(bks[ch],),
                         tag=f"{pfx} pm{mt}.{half}.{ch}.{ct}")
            for ch in range(2):
                sl = slice(half * 1024 + ch * 512, half * 1024 + (ch + 1) * 512)
                evac(sl, PS[bks[ch]][:], bks[ch])

    winS.load(0, b=0)
    for hc in range(4):
        mts = {"q": (2 * hc, 2 * hc + 1), "k": (8 + 2 * hc, 9 + 2 * hc),
               "v": (16 + 2 * hc, 17 + 2 * hc)}
        winS.load(mts["q"][1], b=1)
        for j in range(2):
            mt = mts["q"][j]
            def qe(sl, ps, pk, j=j, mt=mt):
                k.v("tensor_scalar_add", QT2[:, j * T + sl.start: j * T + sl.stop],
                    ps, bin_sb[:, mt:mt + 1], r=(pk, bin_key), w=(f"{pfx}.QT2",),
                    tag=f"{pfx} qe{mt}.{sl.start}")
            proj(mt, j, qe)
            winS.load(mts["k"][j], b=j)
        for j in range(2):
            mt = mts["k"][j]
            def ke(sl, ps, pk, j=j, mt=mt):
                k.v("tensor_scalar_add", KT2[:, j * T + sl.start: j * T + sl.stop],
                    ps, bin_sb[:, mt:mt + 1], r=(pk, bin_key), w=(f"{pfx}.KT2",),
                    tag=f"{pfx} ke{mt}.{sl.start}")
            proj(mt, j, ke)
            winS.load(mts["v"][j], b=j)
        for j in range(2):
            mt = mts["v"][j]
            def ve(sl, ps, pk, j=j, mt=mt):
                half = sl.start // 1024
                ch = (sl.start % 1024) // 512
                k.v("tensor_scalar_add", vtmp[:, ch * 512:(ch + 1) * 512], ps,
                    bin_sb[:, mt:mt + 1], r=(pk, bin_key), w=(f"{pfx}.vtmp",),
                    tag=f"{pfx} ve{mt}.{sl.start}")
                if ch == 1:
                    for tt in range(8):
                        tpar = tt % 2
                        pst = PS[f"psT{tpar}"]
                        k.trans(pst[:, 0:128], vtmp[:, tt * 128:(tt + 1) * 128],
                                k.ident(0, 128), r=(f"{pfx}.vtmp",), w=(f"psT{tpar}",),
                                tag=f"{pfx} vt{mt}.{half}.{tt}")
                        kt = half * 8 + tt
                        k.v("tensor_copy",
                            Vtok[:, kt * 256 + j * 128: kt * 256 + (j + 1) * 128],
                            pst[:, 0:128], r=(f"psT{tpar}",), w=(f"{pfx}.V{kt}",),
                            tag=f"{pfx} vc{mt}.{half}.{tt}")
            proj(mt, j, ve)
            if j == 0:
                pass
            else:
                nhc = hc + 1
                if nhc < 4:
                    winS.load(2 * nhc, b=0)
        _neg_bound(k, pfx, [(QT2[:, 0:T], f"{pfx}.QT2", 0, 128),
                            (QT2[:, T:2 * T], f"{pfx}.QT2", 0, 128),
                            (KT2[:, 0:T], f"{pfx}.KT2", 0, 128),
                            (KT2[:, T:2 * T], f"{pfx}.KT2", 0, 128)],
                   mrows, nbf, negBc, 0, SC, sqb)
        for qc in range(NQC):
            for kt in range(NTT):
                slot = kt % 2
                psS = PS["psA" + str(slot)]
                abk = f"{pfx}.ab{slot}"
                av = abuf[:, slot * 512:(slot + 1) * 512]
                for j in range(2):
                    k.mm(psS[:], KT2[:, j * T + kt * 128: j * T + (kt + 1) * 128],
                         QT2[:, j * T + qc * 512: j * T + (qc + 1) * 512],
                         start=(j == 0), stop=(j == 1),
                         r=(f"{pfx}.QT2", f"{pfx}.KT2"), w=(f"psA{slot}",),
                         tag=f"{pfx} S{hc}.{qc}.{kt}.{j}")
                k.act(av, psS[:], AF.Exp, bias=negBc[:, 0:1], scale=k.cc(SC),
                      r=(f"psA{slot}", f"{pfx}.negBc0"), w=(abk,),
                      tag=f"{pfx} E{hc}.{qc}.{kt}")
                k.mm(PS["psC0"][0:1, :], k.ones_col[:], av, start=(kt == 0),
                     stop=(kt == NTT - 1), r=(abk, "c.ones_col"), w=("psC0",),
                     tag=f"{pfx} dn{hc}.{qc}.{kt}")
                for j in range(2):
                    k.mm(PS["psB" + str(j)][:],
                         Vtok[:, kt * 256 + j * 128: kt * 256 + (j + 1) * 128], av,
                         start=(kt == 0), stop=(kt == NTT - 1),
                         r=(abk, f"{pfx}.V{kt}"), w=(f"psB{j}",),
                         tag=f"{pfx} av{hc}.{qc}.{kt}.{j}")
            k.v("reciprocal", recb[0:1, :], PS["psC0"][0:1, :], r=("psC0",),
                w=(f"{pfx}.recb",), tag=f"{pfx} rc{hc}.{qc}", lowp=True)
            k.mm(PS["psC1"][:], k.ones_row[:], recb[0:1, :], True, True,
                 r=(f"{pfx}.recb", "c.ones_row"), w=("psC1",), tag=f"{pfx} rb{hc}.{qc}")
            k.v("tensor_copy", recbc[:], PS["psC1"][:], r=("psC1",),
                w=(f"{pfx}.recbc",), tag=f"{pfx} rv{hc}.{qc}")
            for j in range(2):
                k.v("tensor_mul",
                    YT[:, (2 * hc + j) * T + qc * 512:(2 * hc + j) * T + (qc + 1) * 512],
                    PS["psB" + str(j)][:], recbc[:],
                    r=(f"psB{j}", f"{pfx}.recbc"), w=(yt_key,),
                    tag=f"{pfx} ym{hc}.{qc}.{j}")


def emit_l2_attn_tail(k, W, xln, xap, xkey, YTm, vecs, mrows, nbf, sqb,
                      x_in, x_out):
    T, PS = k.T, k.PS
    mark2 = k.arena_mark()
    wcaS = WStream(k, "l2awca", W["l2a_wca"], W["wca_meta"])
    YTca = k.sb("l2a_YTca", [128, NCT * T], BF16)
    _mha256(k, "l2ca", xap, xkey, wcaS, vecs["bca"], "l2a.bca", YTca, "l2a.ytca",
            mrows, nbf, sqb)
    wcaoS = WStream(k, "l2awcao", W["l2a_wcao"], W["wcao_meta"])
    ctmp = k.sb("l2a_ctmp", [128, 512], BF16)

    def caevac(mt, half, ps_list, pkeys):
        for ch in range(2):
            sl = slice(mt * T + half * 1024 + ch * 512, mt * T + half * 1024 + (ch + 1) * 512)
            k.v("tensor_scalar", ctmp[:], ps_list[ch], vecs["bcao"][:, mt:mt + 1],
                0.7, ALU.add, ALU.mult, r=(pkeys[ch], "l2a.bcao"), w=("l2a.ctmp",),
                tag=f"l2a cae{mt}.{half}.{ch}")
            k.v("scalar_tensor_tensor", YTm[:, sl], YTm[:, sl], 0.3, ctmp[:],
                ALU.mult, ALU.add, r=("l2a.ytm", "l2a.ctmp"), w=("l2a.ytm",),
                tag=f"l2a cax{mt}.{half}.{ch}")
    emit_gemm(k, wcaoS, lambda ct, sl: YTca[:, ct * T + sl.start:ct * T + sl.stop],
              lambda ct: ("l2a.ytca",), caevac, "l2a_cao")
    k.arena_release(mark2)

    wmcS = WStream(k, "l2awmc", W["l2a_wmc"], W["wmc_meta"])
    YTmc = k.sb("l2a_YTmc", [128, NCT * T], BF16)
    _mha256(k, "l2mc", lambda ct, sl: YTm[:, ct * T + sl.start:ct * T + sl.stop],
            lambda ct: ("l2a.ytm",), wmcS, vecs["bmc"], "l2a.bmc", YTmc, "l2a.ytmc",
            mrows, nbf, sqb)
    wmcoS = WStream(k, "l2awmco", W["l2a_wmco"], W["wmco_meta"])
    ctmp2 = k.sb("l2a_ctmp2", [128, 512], BF16)

    def mcevac(mt, half, ps_list, pkeys):
        for ch in range(2):
            sl = slice(mt * T + half * 1024 + ch * 512, mt * T + half * 1024 + (ch + 1) * 512)
            k.v("tensor_scalar", ctmp2[:], ps_list[ch], vecs["bmco"][:, mt:mt + 1],
                0.15, ALU.add, ALU.mult, r=(pkeys[ch], "l2a.bmco"), w=("l2a.ctmp2",),
                tag=f"l2a mce{mt}.{half}.{ch}")
            k.v("scalar_tensor_tensor", YTm[:, sl], YTm[:, sl], 0.85, ctmp2[:],
                ALU.mult, ALU.add, r=("l2a.ytm", "l2a.ctmp2"), w=("l2a.ytm",),
                tag=f"l2a mcx{mt}.{half}.{ch}")
    emit_gemm(k, wmcoS, lambda ct, sl: YTmc[:, ct * T + sl.start:ct * T + sl.stop],
              lambda ct: ("l2a.ytmc",), mcevac, "l2a_mco")

    woS = WStream(k, "l2awo", W["l2a_wo"], W["wo_meta"])
    return emit_out_resid(k, woS,
                          lambda ct, sl: YTm[:, ct * T + sl.start:ct * T + sl.stop],
                          lambda ct: ("l2a.ytm",), x_in, x_out, "l2a",
                          bias_sb=vecs["bo"], bias_key="l2a.bo")



class StreamCursor:

    def __init__(self, ws, order):
        self.ws, self.order, self.cur = ws, order, 0

    def use(self, idx):
        while self.cur <= min(idx + 1, len(self.order) - 1):
            self.ws.load(self.order[self.cur], b=self.cur % self.ws.nbuf)
            self.cur += 1
        return self.cur

    def tile(self, idx, ct):
        return self.ws.tile(self.order[idx], ct, b=idx % self.ws.nbuf)


def cat_tiles(ws_list):
    arrs = [a for a, m in ws_list]
    m0 = ws_list[0][1]
    n_m, n_c, widths, C = m0
    return (np.concatenate(arrs),
            (n_m * len(ws_list), n_c, list(widths) * len(ws_list), C))



def prep_l2_ffn(inputs):
    arrs, meta = {}, {}
    arr, m = host_tiles(np.asarray(inputs["L2_ffn_Wg"], np.float32).T)
    arrs["l2f_wg"] = arr
    meta["wg_meta"] = m
    for nm, w in (("ws1", np.asarray(inputs["L2_ffn_Ws1"], np.float32).T),
                  ("ws2", np.asarray(inputs["L2_ffn_Ws2"], np.float32).T),
                  ("ws3", np.asarray(inputs["L2_ffn_Ws3"], np.float32).T)):
        arr, m = host_tiles(w)
        arrs["l2f_" + nm] = arr
        meta[nm + "_meta"] = m
    W1 = np.asarray(inputs["L2_ffn_W1"], np.float32)
    W2 = np.asarray(inputs["L2_ffn_W2"], np.float32)
    W3 = np.asarray(inputs["L2_ffn_W3"], np.float32)
    arrs["l2f_w1"], meta["w1_meta"] = cat_tiles([host_tiles(W1[e].T) for e in range(8)])
    arrs["l2f_w2"], meta["w2_meta"] = cat_tiles([host_tiles(W2[e].T) for e in range(8)])
    arrs["l2f_w3"], meta["w3_meta"] = cat_tiles([host_tiles(W3[e].T) for e in range(8)])
    arrs["l2f_rb"] = np.asarray(inputs["L2_ffn_routing_bias"], np.float32).reshape(8, 1)
    return arrs, meta


def _route_topk(k, pfx, logT, logkey, wtok, K_top, extras=None):
    T, PS = k.T, k.PS
    NTT = T // 128
    ltok = k.sb(pfx + "_ltok", [128, 8], F32)
    rt = k.sb(pfx + "_rt", [128, 6 * 8], F32)
    rc = k.sb(pfx + "_rc", [128, 4], F32)
    _route_topk.last_rt = rt
    for tt in range(NTT):
        tpar = tt % 2
        pst = PS[f"psT{tpar}"]
        k.trans(pst[0:128, 0:8], logT[0:8, tt * 128:(tt + 1) * 128], k.ident(0, 8),
                r=(logkey,), w=(f"psT{tpar}",), tag=f"{pfx} lt{tt}")
        k.v("tensor_copy", ltok[:], pst[0:128, 0:8], r=(f"psT{tpar}",),
            w=(f"{pfx}.ltok",), tag=f"{pfx} lc{tt}")
        k.v("tensor_copy", rt[:, 0:8], ltok[:], r=(f"{pfx}.ltok",), w=(f"{pfx}.cur",),
            tag=f"{pfx} cur{tt}")
        k.v("tensor_scalar_mul", rt[:, 8:16], ltok[:], 0.0, r=(f"{pfx}.ltok",),
            w=(f"{pfx}.kept",), tag=f"{pfx} kz{tt}")
        for r_ in range(K_top):
            k.v("reduce_max", rc[:, 0:1], rt[:, 0:8], axis=AX, r=(f"{pfx}.cur",),
                w=(f"{pfx}.mx",), tag=f"{pfx} mx{tt}.{r_}")
            if r_ == 0:
                k.v("tensor_copy", rc[:, 1:2], rc[:, 0:1], r=(f"{pfx}.mx",),
                    w=(f"{pfx}.m1",), tag=f"{pfx} m1{tt}")
            k.v("tensor_single_scalar", rt[:, 16:24], rt[:, 0:8], rc[:, 0:1],
                ALU.is_equal, r=(f"{pfx}.cur", f"{pfx}.mx"), w=(f"{pfx}.eq",),
                tag=f"{pfx} eq{tt}.{r_}")
            k.v("tensor_add", rt[:, 8:16], rt[:, 8:16], rt[:, 16:24],
                r=(f"{pfx}.kept", f"{pfx}.eq"), w=(f"{pfx}.kept",),
                tag=f"{pfx} ka{tt}.{r_}")
            if r_ + 1 < K_top:
                k.v("scalar_tensor_tensor", rt[:, 0:8], rt[:, 16:24], -1e9,
                    rt[:, 0:8], ALU.mult, ALU.add, r=(f"{pfx}.eq", f"{pfx}.cur"),
                    w=(f"{pfx}.cur",), tag=f"{pfx} ms{tt}.{r_}")
        k.v("tensor_single_scalar", rt[:, 24:32], ltok[:], rc[:, 1:2], ALU.subtract,
            r=(f"{pfx}.ltok", f"{pfx}.m1"), w=(f"{pfx}.sh",), tag=f"{pfx} sh{tt}")
        k.act(rt[:, 32:40], rt[:, 24:32], AF.Exp, r=(f"{pfx}.sh",), w=(f"{pfx}.ex",),
              tag=f"{pfx} ex{tt}")
        k.v("tensor_mul", rt[:, 32:40], rt[:, 32:40], rt[:, 8:16],
            r=(f"{pfx}.ex", f"{pfx}.kept"), w=(f"{pfx}.ex",), tag=f"{pfx} exk{tt}")
        k.v("reduce_sum", rc[:, 2:3], rt[:, 32:40], axis=AX, r=(f"{pfx}.ex",),
            w=(f"{pfx}.dn",), tag=f"{pfx} dn{tt}")
        k.v("reciprocal", rc[:, 3:4], rc[:, 2:3], r=(f"{pfx}.dn",), w=(f"{pfx}.rd",),
            tag=f"{pfx} rd{tt}")
        k.v("tensor_single_scalar", wtok[:, tt * 8:(tt + 1) * 8], rt[:, 32:40],
            rc[:, 3:4], ALU.mult, r=(f"{pfx}.ex", f"{pfx}.rd"), w=(f"{pfx}.wtok",),
            tag=f"{pfx} wt{tt}")
        if extras is not None:
            extras(tt)


def _wtok_to_rows(k, pfx, wtok, wT):
    T, PS = k.T, k.PS
    NTT = T // 128
    for tt in range(NTT):
        tpar = tt % 2
        pst = PS[f"psT{tpar}"]
        k.trans(pst[0:8, 0:128], wtok[:, tt * 8:(tt + 1) * 8], k.ident(0, 128),
                r=(f"{pfx}.wtok",), w=(f"psT{tpar}",), tag=f"{pfx} wtt{tt}")
        k.v("tensor_copy", wT[0:8, tt * 128:(tt + 1) * 128], pst[0:8, 0:128],
            r=(f"psT{tpar}",), w=(f"{pfx}.wT",), tag=f"{pfx} wtc{tt}")


def emit_l2_ffn(k, x_in, x_out, W):
    T, PS = k.T, k.PS
    NTT = T // 128
    xln = k.sb("l2f_xln", [128, NCT * T], BF16)
    emit_norm(k, x_in, xln, "ln", pfx="l2f")
    xap = lambda ct, sl: xln[:, ct * T + sl.start: ct * T + sl.stop]
    xkey = lambda ct: (f"l2f.xb{ct}",)

    rb = k.sb("l2f_rb", [8, 1], F32)
    k.dmac(rb[:], W["l2f_rb"].ap(), w=("l2f.rb",), tag="l2f rb")
    wgS = WStream(k, "l2fwg", W["l2f_wg"], W["wg_meta"])
    logT = k.sb("l2f_logT", [8, T], BF16)

    def gevac(mt, half, ps_list, pkeys):
        for ch in range(2):
            sl = slice(half * 1024 + ch * 512, half * 1024 + (ch + 1) * 512)
            k.v("tensor_scalar_add", logT[:, sl], ps_list[ch], rb[:],
                r=(pkeys[ch], "l2f.rb"), w=("l2f.logT",), tag=f"l2f ge{half}.{ch}")
    emit_gemm(k, wgS, xap, xkey, gevac, "l2f_g8")
    wtok = k.sb("l2f_wtok", [128, NTT * 8], BF16)
    _route_topk(k, "l2f", logT, "l2f.logT", wtok, 4)
    wT = k.sb("l2f_wT", [8, T], BF16)
    _wtok_to_rows(k, "l2f", wtok, wT)

    s1S = WStream(k, "l2fs1", W["l2f_ws1"], W["ws1_meta"])
    s2S = WStream(k, "l2fs2", W["l2f_ws2"], W["ws2_meta"])
    s3S = WStream(k, "l2fs3", W["l2f_ws3"], W["ws3_meta"])
    w1S = WStream(k, "l2fw1", W["l2f_w1"], W["w1_meta"])
    w2S = WStream(k, "l2fw2", W["l2f_w2"], W["w2_meta"])
    w3S = WStream(k, "l2fw3", W["l2f_w3"], W["w3_meta"])
    NH1 = s1S.n_m
    c1 = StreamCursor(s1S, [mt for tc in range(2) for mt in range(NH1)])
    c2 = StreamCursor(s2S, [mt for tc in range(2) for mt in range(NH1)])
    c3 = StreamCursor(s3S, [mt for tc in range(2) for mt in range(8)])
    e1 = StreamCursor(w1S, [e * NH1 + mt for tc in range(2) for e in range(8)
                            for mt in range(NH1)])
    e2 = StreamCursor(w2S, [e * NH1 + mt for tc in range(2) for e in range(8)
                            for mt in range(NH1)])
    e3 = StreamCursor(w3S, [e * 8 + mt for tc in range(2) for e in range(8)
                            for mt in range(8)])

    hv = k.sb("l2f_hv", [128, NH1 * 1024], BF16)
    acc = k.sb("l2f_acc", [128, 8 * 1024], F32)
    wbc = k.sb("l2f_wbc", [128, 1024], BF16)
    wrow = k.sb("l2f_wrow", [1, 1024], BF16)
    stmp = k.sb("l2f_stmp", [128, 512], BF16)
    xfp = k.sb("l2f_xfp", [128, 1024], F32)
    xo = k.sb("l2f_xo", [128, 1024], F32)

    def swiglu_pass(tc, cu1, cu2, cu3, i1base, scale_mode, e=None):
        tsl0 = tc * 1024
        for mtt in range(NH1):
            i1 = i1base + mtt
            cu1.use(i1), cu2.use(i1)
            w = s1S.widths[mtt]
            for ct in range(NCT):
                st1, k1 = cu1.tile(i1, ct)
                for ch in range(2):
                    bk = ("psA0", "psA1")[ch]
                    sl = slice(tsl0 + ch * 512, tsl0 + (ch + 1) * 512)
                    k.mm(PS[bk][0:w, :], st1, xap(ct, sl), start=(ct == 0),
                         stop=(ct == NCT - 1), r=(k1,) + xkey(ct), w=(bk,),
                         tag=f"l2f h1 {tc}.{e}.{mtt}.{ch}")
            for ct in range(NCT):
                st2, k2 = cu2.tile(i1, ct)
                for ch in range(2):
                    bk = ("psC0", "psC1")[ch]
                    sl = slice(tsl0 + ch * 512, tsl0 + (ch + 1) * 512)
                    k.mm(PS[bk][0:w, :], st2, xap(ct, sl), start=(ct == 0),
                         stop=(ct == NCT - 1), r=(k2,) + xkey(ct), w=(bk,),
                         tag=f"l2f h2 {tc}.{e}.{mtt}.{ch}")
            for ch in range(2):
                bkA = ("psA0", "psA1")[ch]
                bkC = ("psC0", "psC1")[ch]
                k.act(stmp[0:w, :], PS[bkA][0:w, :], AF.Silu, r=(bkA,),
                      w=("l2f.stmp",), tag=f"l2f sg {tc}.{e}.{mtt}.{ch}")
                k.v("tensor_mul", hv[0:w, mtt * 1024 + ch * 512: mtt * 1024 + (ch + 1) * 512],
                    stmp[0:w, :], PS[bkC][0:w, :], r=("l2f.stmp", bkC),
                    w=("l2f.hv",), tag=f"l2f hv {tc}.{e}.{mtt}.{ch}")
        for mtd in range(8):
            i3 = (i1base // NH1) * 8 + mtd
            cu3.use(i3)
            for hct in range(NH1):
                st3, k3 = cu3.tile(i3, hct)
                cl = min(128, 2730 - hct * 128)
                st3v = st3[0:cl, :]
                for ch in range(2):
                    bk = ("psB0", "psB1")[ch]
                    k.mm(PS[bk][:], st3v,
                         hv[0:cl, hct * 1024 + ch * 512: hct * 1024 + (ch + 1) * 512],
                         start=(hct == 0), stop=(hct == NH1 - 1),
                         r=(k3, "l2f.hv"), w=(bk,), tag=f"l2f o {tc}.{e}.{mtd}.{ch}")
            for ch in range(2):
                bk = ("psB0", "psB1")[ch]
                dst = acc[:, mtd * 1024 + ch * 512: mtd * 1024 + (ch + 1) * 512]
                if scale_mode == "init":
                    k.v("tensor_copy", dst, PS[bk][:], r=(bk,), w=("l2f.acc",),
                        tag=f"l2f ai {tc}.{e}.{mtd}.{ch}")
                else:
                    k.v("tensor_mul", xo[:, ch * 512:(ch + 1) * 512], PS[bk][:],
                        wbc[:, ch * 512:(ch + 1) * 512], r=(bk, "l2f.wbc"),
                        w=("l2f.t",), tag=f"l2f am {tc}.{e}.{mtd}.{ch}")
                    k.v("tensor_add", dst, dst, xo[:, ch * 512:(ch + 1) * 512],
                        r=("l2f.acc", "l2f.t"), w=("l2f.acc",),
                        tag=f"l2f aa {tc}.{e}.{mtd}.{ch}")

    for tc in range(2):
        swiglu_pass(tc, c1, c2, c3, tc * NH1, "init")
        for e in range(8):
            k.dmac(wrow[:], wT[e:e + 1, tc * 1024:(tc + 1) * 1024],
                   r=("l2f.wT",), w=("l2f.wrow",), tag=f"l2f wr{tc}.{e}")
            for ch in range(2):
                k.mm(PS["psC0"][:], k.ones_row[:], wrow[0:1, ch * 512:(ch + 1) * 512],
                     True, True, r=("l2f.wrow", "c.ones_row"), w=("psC0",),
                     tag=f"l2f wb{tc}.{e}.{ch}")
                k.v("tensor_copy", wbc[:, ch * 512:(ch + 1) * 512], PS["psC0"][:],
                    r=("psC0",), w=("l2f.wbc",), tag=f"l2f wbe{tc}.{e}.{ch}")
            swiglu_pass(tc, e1, e2, e3, (tc * 8 + e) * NH1, "scaled", e=e)
        for mt in range(8):
            k.dmac(xfp[:], x_in.ap()[mt * 128:(mt + 1) * 128, tc * 1024:(tc + 1) * 1024],
                   w=("l2f.xfp",), tag=f"l2f xl{tc}.{mt}")
            k.v("tensor_add", xo[:], xfp[:], acc[:, mt * 1024:(mt + 1) * 1024],
                r=("l2f.xfp", "l2f.acc"), w=("l2f.t",), tag=f"l2f fa{tc}.{mt}")
            k.dmac(x_out.ap()[mt * 128:(mt + 1) * 128, tc * 1024:(tc + 1) * 1024], xo[:],
                   r=("l2f.t",), w=(f"xout.l2f.{mt}.{tc}",), tag=f"l2f xs{tc}.{mt}")
    return [f"xout.l2f.{mt}.{tc}" for mt in range(8) for tc in range(2)]




def prep_l1_ffn(inputs):
    arrs, meta = {}, {}
    arr, m = host_tiles(np.asarray(inputs["L1_ffn_Wr"], np.float32))
    arrs["l1f_wr"] = arr
    meta["wr_meta"] = m
    W1 = np.asarray(inputs["L1_ffn_W1"], np.float32)
    W2 = np.asarray(inputs["L1_ffn_W2"], np.float32)
    arrs["l1f_w1"], meta["w1_meta"] = cat_tiles([host_tiles(W1[e]) for e in range(8)])
    arrs["l1f_w2"], meta["w2_meta"] = cat_tiles([host_tiles(W2[e]) for e in range(8)])
    arrs["l1f_br"] = np.asarray(inputs["L1_ffn_br"], np.float32).reshape(8, 1)
    b1 = np.asarray(inputs["L1_ffn_b1"], np.float32).reshape(8, 32, 128)
    arrs["l1f_b1"] = np.ascontiguousarray(b1.transpose(2, 0, 1).reshape(128, 256))
    arrs["l1f_b2"] = np.ascontiguousarray(np.asarray(inputs["L1_ffn_b2"], np.float32))
    arrs["c_ut128"] = np.triu(np.ones((128, 128), np.float32), 1)
    tle = np.zeros((128, 128), np.float32)
    for bp in range(16):
        for ep in range(8):
            for b in range(16):
                if bp < b:
                    tle[bp * 8 + ep, b * 8 + ep] = 1.0
    arrs["c_tle128"] = tle
    cm = np.zeros((NCORES, 8, 1), np.float32)
    for c in range(NCORES):
        cm[c, :c, 0] = 1.0
    arrs["pc8_l1f_coremask"] = cm
    return arrs, meta


def emit_l1_ffn(k, x_in, x_out, W):
    T, PS = k.T, k.PS
    NTT = T // 128
    CAP = 4096.0
    xln = k.sb("l1f_xln", [128, NCT * T], BF16)
    emit_norm(k, x_in, xln, "ln", pfx="l1f")
    xap = lambda ct, sl: xln[:, ct * T + sl.start: ct * T + sl.stop]
    xkey = lambda ct: (f"l1f.xb{ct}",)

    br = k.sb("l1f_br", [8, 1], F32)
    k.dmac(br[:], W["l1f_br"].ap(), w=("l1f.br",), tag="l1f br")
    ut128 = k.sb("c_ut128", [128, 128], BF16)
    k.dmacast(ut128[:], W["c_ut128"].ap(), w=("c.ut128",), tag="l1f ut")
    tle = k.sb("c_tle128", [128, 128], BF16)
    k.dmacast(tle[:], W["c_tle128"].ap(), w=("c.tle",), tag="l1f tle")
    cmask = k.sb("l1f_cmask", [8, 1], F32)
    k.dmac(cmask[:], W["pc8_l1f_coremask"].ap(), w=("l1f.cmask",), tag="l1f cm")
    ones_row_f = k.sb("l1f_ones_f", [1, 128], F32)
    k.emit("gpsimd", lambda e: e.memset(ones_row_f[:], 1.0), w=("l1f.onesf",),
           tag="l1f onesf")

    wrS = WStream(k, "l1fwr", W["l1f_wr"], W["wr_meta"])
    logT = k.sb("l1f_logT", [8, T], BF16)

    def gevac(mt, half, ps_list, pkeys):
        for ch in range(2):
            sl = slice(half * 1024 + ch * 512, half * 1024 + (ch + 1) * 512)
            k.v("tensor_scalar_add", logT[:, sl], ps_list[ch], br[:],
                r=(pkeys[ch], "l1f.br"), w=("l1f.logT",), tag=f"l1f ge{half}.{ch}")
    emit_gemm(k, wrS, xap, xkey, gevac, "l1f_g8")

    wtok = k.sb("l1f_wtok", [128, NTT * 8], BF16)
    seltok = k.sb("l1f_sel", [128, NTT * 8], BF16)

    def extras(tt):
        k.v("tensor_copy", seltok[:, tt * 8:(tt + 1) * 8],
            _route_topk.last_rt[:, 8:16],
            r=("l1f.kept",), w=("l1f.sel",), tag=f"l1f selc{tt}")
    _route_topk(k, "l1f", logT, "l1f.logT", wtok, 2, extras=extras)

    bs_sb = k.sb("l1f_bs", [1, 128], BF16)
    bscol = k.sb("l1f_bscol", [128, 1], BF16)
    prow = k.sb("l1f_prow", [1, 128], F32)
    cnt = k.sb("l1f_cnt", [1, 8], F32)
    cgat = k.sb("l1f_cgat", [8, 8], F32)
    coff = k.sb("l1f_coff", [1, 8], F32)
    obr = k.sb("l1f_obr", [1, 8], F32)
    rtile = k.sb("l1f_rtile", [128, 8], F32)
    rg = k.sb("l1f_rg", [128, 8], F32)
    bbuf = k.dram("l1f_bb", [128, 128], F32)
    bbflat = bass.AP(bbuf, 0, [[1, 1], [1, 16384]])
    zz = k.sb("l1f_zz", [128, 128], F32)

    for tt in range(NTT):
        k.mm(PS["psC0"][0:1, tt * 8:(tt + 1) * 8], k.ones_col[:],
             seltok[:, tt * 8:(tt + 1) * 8], True, True,
             r=("l1f.sel", "c.ones_col"), w=("psC0",), tag=f"l1f bs{tt}")
        k.mm(PS["psC1"][0:1, 0:8], k.ones_col[:], seltok[:, tt * 8:(tt + 1) * 8],
             start=(tt == 0), stop=(tt == NTT - 1),
             r=("l1f.sel", "c.ones_col"), w=("psC1",), tag=f"l1f cnt{tt}")
    k.v("tensor_copy", bs_sb[:], PS["psC0"][0:1, 0:128], r=("psC0",), w=("l1f.bs",),
        tag="l1f bse")
    k.v("tensor_copy", cnt[:], PS["psC1"][0:1, 0:8], r=("psC1",), w=("l1f.cnt",),
        tag="l1f cnte")
    k.emit("gpsimd", lambda e: e.memset(zz[:], 0.0), w=("l1f.zz",), tag="l1f zz")
    k.dmac(bbuf.ap(), zz[:], r=("l1f.zz",), w=("l1f.bb",), q="gpsimd", tag="l1f bz")

    def wrrow(e):
        pid = e.partition_id()
        sem, amt = k._active_sem
        for cc in range(NCORES):
            with e.If(pid == cc):
                e.dma_start(bbuf.ap()[cc:cc + 1, 0:8], cnt[:]).then_inc(sem, amt)
        return "incs_attached"
    k.emit("gpsimd", wrrow, r=("l1f.cnt", "l1f.bb"), w=("l1f.bb",), dma=True,
           tag="l1f ccw")
    k.emit("gpsimd",
           lambda e: e.collective_compute("AllReduce", ALU.add,
                                          [[i for i in range(NCORES)]],
                                          [bbflat], [bbflat]),
           r=("l1f.bb",), w=("l1f.bb",), top=True, tag="l1f ar")
    k.dmac(cgat[:], bbuf.ap()[0:8, 0:8], r=("l1f.bb",), w=("l1f.cgat",), tag="l1f ccr")
    k.mm(PS["psC0"][0:1, 0:8], cmask[:], cgat[:], True, True,
         r=("l1f.cgat", "l1f.cmask"), w=("psC0",), tag="l1f coffmm")
    k.v("tensor_copy", coff[:], PS["psC0"][0:1, 0:8], r=("psC0",), w=("l1f.coff",),
        tag="l1f coffe")
    k.trans(PS["psT0"][0:128, 0:1], bs_sb[0:1, :], k.ident(0, 1),
            r=("l1f.bs",), w=("psT0",), tag="l1f bst")
    k.v("tensor_copy", bscol[:], PS["psT0"][0:128, 0:1], r=("psT0",), w=("l1f.bscol",),
        tag="l1f bsc")
    k.mm(PS["psC0"][0:1, 0:128], bscol[:], tle[:], True, True,
         r=("l1f.bscol", "c.tle"), w=("psC0",), tag="l1f pmm")
    k.v("tensor_copy", prow[:], PS["psC0"][0:1, 0:128], r=("psC0",), w=("l1f.prow",),
        tag="l1f pev")
    for tt in range(NTT):
        k.mm(PS["psC0"][0:128, 0:8], ut128[:], seltok[:, tt * 8:(tt + 1) * 8],
             True, True, r=("l1f.sel", "c.ut128"), w=("psC0",), tag=f"l1f rk{tt}")
        k.v("tensor_copy", rtile[:], PS["psC0"][0:128, 0:8], r=("psC0",),
            w=("l1f.rtile",), tag=f"l1f rke{tt}")
        k.v("tensor_add", obr[:], prow[0:1, tt * 8:(tt + 1) * 8], coff[:],
            r=("l1f.prow", "l1f.coff"), w=("l1f.obr",), tag=f"l1f ob{tt}")
        k.mm(PS["psC1"][0:128, 0:8], ones_row_f[:], obr[:], True, True,
             r=("l1f.obr", "l1f.onesf"), w=("psC1",), tag=f"l1f obc{tt}")
        k.v("tensor_add", rg[:], rtile[:], PS["psC1"][0:128, 0:8],
            r=("l1f.rtile", "psC1"), w=("l1f.rg",), tag=f"l1f rg{tt}")
        k.v("tensor_single_scalar", rg[:], rg[:], CAP, ALU.is_lt,
            r=("l1f.rg",), w=("l1f.rg",), tag=f"l1f kp{tt}")
        k.v("tensor_mul", rg[:], rg[:], seltok[:, tt * 8:(tt + 1) * 8],
            r=("l1f.rg", "l1f.sel"), w=("l1f.rg",), tag=f"l1f kp2{tt}")
        k.v("tensor_mul", wtok[:, tt * 8:(tt + 1) * 8], wtok[:, tt * 8:(tt + 1) * 8],
            rg[:], r=("l1f.rg", "l1f.wtok"), w=("l1f.wtok",), tag=f"l1f wf{tt}")
    wT = k.sb("l1f_wT", [8, T], BF16)
    _wtok_to_rows(k, "l1f", wtok, wT)

    b1p = k.sb("l1f_b1", [128, 256], F32)
    k.dmac(b1p[:], W["l1f_b1"].ap(), w=("l1f.b1",), tag="l1f b1")
    b2sb = k.sb("l1f_b2", [8, 1024], BF16)
    k.dmacast(b2sb[:], W["l1f_b2"].ap(), w=("l1f.b2",), tag="l1f b2")
    w1S = WStream(k, "l1fw1", W["l1f_w1"], W["w1_meta"])
    w2S = WStream(k, "l1fw2", W["l1f_w2"], W["w2_meta"])
    e1 = StreamCursor(w1S, [e * 32 + m for tc in range(2) for e in range(8)
                            for m in range(32)])
    e2 = StreamCursor(w2S, [e * 8 + m for tc in range(2) for e in range(8)
                            for m in range(8)])
    hid = k.sb("l1f_hid", [128, 32 * 1024], BF16)
    acc = k.sb("l1f_acc", [128, 8 * 1024], F32)
    wbc = k.sb("l1f_wbc", [128, 1024], BF16)
    wrow = k.sb("l1f_wrow", [1, 1024], BF16)
    xfp = k.sb("l1f_xfp", [128, 1024], F32)
    xo = k.sb("l1f_xo", [128, 1024], F32)

    for tc in range(2):
        tsl0 = tc * 1024
        for mtd in range(8):
            for ch in range(2):
                k.mm(PS["psB0"][:], b2sb[0:8, mtd * 128:(mtd + 1) * 128],
                     wT[0:8, tsl0 + ch * 512: tsl0 + (ch + 1) * 512], True, True,
                     r=("l1f.b2", "l1f.wT"), w=("psB0",), tag=f"l1f sd{tc}.{mtd}.{ch}")
                k.v("tensor_copy", acc[:, mtd * 1024 + ch * 512: mtd * 1024 + (ch + 1) * 512],
                    PS["psB0"][:], r=("psB0",), w=("l1f.acc",),
                    tag=f"l1f sde{tc}.{mtd}.{ch}")
        for e in range(8):
            k.dmac(wrow[:], wT[e:e + 1, tsl0:tsl0 + 1024], r=("l1f.wT",),
                   w=("l1f.wrow",), tag=f"l1f wr{tc}.{e}")
            for ch in range(2):
                k.mm(PS["psC0"][:], k.ones_row[:], wrow[0:1, ch * 512:(ch + 1) * 512],
                     True, True, r=("l1f.wrow", "c.ones_row"), w=("psC0",),
                     tag=f"l1f wb{tc}.{e}.{ch}")
                k.v("tensor_copy", wbc[:, ch * 512:(ch + 1) * 512], PS["psC0"][:],
                    r=("psC0",), w=("l1f.wbc",), tag=f"l1f wbe{tc}.{e}.{ch}")
            for mtt in range(32):
                i1 = (tc * 8 + e) * 32 + mtt
                e1.use(i1)
                for ct in range(NCT):
                    st1, k1 = e1.tile(i1, ct)
                    for ch in range(2):
                        bk = ("psA0", "psA1")[ch]
                        sl = slice(tsl0 + ch * 512, tsl0 + (ch + 1) * 512)
                        k.mm(PS[bk][:], st1, xap(ct, sl), start=(ct == 0),
                             stop=(ct == NCT - 1), r=(k1,) + xkey(ct), w=(bk,),
                             tag=f"l1f h{tc}.{e}.{mtt}.{ch}")
                for ch in range(2):
                    bk = ("psA0", "psA1")[ch]
                    k.act(hid[:, mtt * 1024 + ch * 512: mtt * 1024 + (ch + 1) * 512],
                          PS[bk][:], AF.Gelu, bias=b1p[:, e * 32 + mtt:e * 32 + mtt + 1],
                          r=(bk, "l1f.b1"), w=("l1f.hid",), tag=f"l1f hg{tc}.{e}.{mtt}.{ch}")
            for mtd in range(8):
                i2 = (tc * 8 + e) * 8 + mtd
                e2.use(i2)
                for hct in range(32):
                    st2, k2 = e2.tile(i2, hct)
                    for ch in range(2):
                        bk = ("psB0", "psB1")[ch]
                        k.mm(PS[bk][:], st2,
                             hid[:, hct * 1024 + ch * 512: hct * 1024 + (ch + 1) * 512],
                             start=(hct == 0), stop=(hct == 31),
                             r=(k2, "l1f.hid"), w=(bk,), tag=f"l1f o{tc}.{e}.{mtd}.{ch}")
                for ch in range(2):
                    bk = ("psB0", "psB1")[ch]
                    dst = acc[:, mtd * 1024 + ch * 512: mtd * 1024 + (ch + 1) * 512]
                    k.v("tensor_mul", xo[:, ch * 512:(ch + 1) * 512], PS[bk][:],
                        wbc[:, ch * 512:(ch + 1) * 512], r=(bk, "l1f.wbc"),
                        w=("l1f.t",), tag=f"l1f am{tc}.{e}.{mtd}.{ch}")
                    k.v("tensor_add", dst, dst, xo[:, ch * 512:(ch + 1) * 512],
                        r=("l1f.acc", "l1f.t"), w=("l1f.acc",),
                        tag=f"l1f aa{tc}.{e}.{mtd}.{ch}")
        for mt in range(8):
            k.dmac(xfp[:], x_in.ap()[mt * 128:(mt + 1) * 128, tsl0:tsl0 + 1024],
                   w=("l1f.xfp",), tag=f"l1f xl{tc}.{mt}")
            k.v("tensor_add", xo[:], xfp[:], acc[:, mt * 1024:(mt + 1) * 1024],
                r=("l1f.xfp", "l1f.acc"), w=("l1f.t",), tag=f"l1f fa{tc}.{mt}")
            k.dmac(x_out.ap()[mt * 128:(mt + 1) * 128, tsl0:tsl0 + 1024], xo[:],
                   r=("l1f.t",), w=(f"xout.l1f.{mt}.{tc}",), tag=f"l1f xs{tc}.{mt}")
    return [f"xout.l1f.{mt}.{tc}" for mt in range(8) for tc in range(2)]




def prep_l3_attn(inputs):
    arrs, meta = {}, {}
    for nm, w in (("inw", np.asarray(inputs["L3_attn_in_w"], np.float32).T),
                  ("skw", np.asarray(inputs["L3_attn_skip_w"], np.float32).T),
                  ("wd2", np.asarray(inputs["L3_attn_wd2"], np.float32).T),
                  ("outw", np.asarray(inputs["L3_attn_out_w"], np.float32).T)):
        arr, m = host_tiles(np.ascontiguousarray(w))
        arrs["l3a_" + nm] = arr
        meta[nm + "_meta"] = m
    wB = np.asarray(inputs["L3_attn_wB"], np.float32)
    wC = np.asarray(inputs["L3_attn_wC"], np.float32)
    wd1 = np.asarray(inputs["L3_attn_wd1"], np.float32)
    WM = np.zeros((16, 128, 128), np.float32)
    for et in range(16):
        sl = slice(et * 128, (et + 1) * 128)
        WM[et, :, 0:16] = wB.T[sl]
        WM[et, :, 32:48] = wC.T[sl]
        WM[et, :, 64:128] = wd1.T[sl]
    arrs["l3a_wm"] = WM.reshape(-1)
    meta["wm_meta"] = (16, 1, [128] * 16, 128)
    arrs["l3a_alog"] = np.ascontiguousarray(np.asarray(inputs["L3_attn_A_log"], np.float32))
    def pack(v, n):
        return np.ascontiguousarray(np.asarray(v, np.float32).reshape(n, 128).T)
    arrs["l3a_gn"] = pack(inputs["L3_attn_norm_w"], 8)
    arrs["l3a_convb"] = pack(inputs["L3_attn_conv_b"], 16)
    arrs["l3a_wd2b"] = pack(inputs["L3_attn_wd2_b"], 16)
    arrs["l3a_wd"] = pack(inputs["L3_attn_W_D"], 16)
    cw = np.asarray(inputs["L3_attn_conv_w"], np.float32).reshape(16, 128, 4)
    arrs["l3a_convw"] = np.ascontiguousarray(cw.transpose(1, 0, 2).reshape(128, 64))
    return arrs, meta


def emit_l3_attn(k, x_in, x_out, W):
    T, PS = k.T, k.PS
    NE = 16
    xl = k.sb("l3a_xl", [128, NCT * T], BF16)
    emit_norm(k, x_in, xl, "ln", pfx="l3l")
    xn = k.sb("l3a_xn", [128, NCT * T], BF16)
    gn = k.sb("l3a_gn", [128, NCT], F32)
    k.dmac(gn[:], W["l3a_gn"].ap(), w=("l3a.gn",), tag="l3a gn")
    emit_norm(k, ("sb", xl, lambda ct: ("l3l.xln",)), xn, "rms", gain_sb=gn,
              gkey="l3a.gn", eps=1e-5, pfx="l3a")
    xap = lambda ct, sl: xn[:, ct * T + sl.start: ct * T + sl.stop]
    xkey = lambda ct: (f"l3a.xb{ct}",)

    convb = k.sb("l3a_convb", [128, 16], F32)
    k.dmac(convb[:], W["l3a_convb"].ap(), w=("l3a.convb",), tag="l3a convb")
    wd2b = k.sb("l3a_wd2b", [128, 16], F32)
    k.dmac(wd2b[:], W["l3a_wd2b"].ap(), w=("l3a.wd2b",), tag="l3a wd2b")
    wdcol = k.sb("l3a_wd", [128, 16], F32)
    k.dmac(wdcol[:], W["l3a_wd"].ap(), w=("l3a.wd",), tag="l3a wd")
    convw = k.sb("l3a_convw", [128, 64], F32)
    k.dmac(convw[:], W["l3a_convw"].ap(), w=("l3a.convw",), tag="l3a convw")

    xs_d = k.dram("l3a_xs", [2048, T], BF16)
    ss_d = k.dram("l3a_ss", [2048, T], BF16)
    dt_d = k.dram("l3a_dt", [2048, T], BF16)
    dtx_d = k.dram("l3a_dtx", [2048, T], BF16)
    y_d = k.dram("l3a_y", [2048, T], F32)

    inwS = WStream(k, "l3ainw", W["l3a_inw"], W["inw_meta"])
    skwS = WStream(k, "l3askw", W["l3a_skw"], W["skw_meta"])
    wmS = WStream(k, "l3awm", W["l3a_wm"], W["wm_meta"], nbuf=2)
    wd2S = WStream(k, "l3awd2", W["l3a_wd2"], W["wd2_meta"], nbuf=1)

    BmT = k.sb("l3a_BmT", [16, T], BF16)
    CmT = k.sb("l3a_CmT", [16, T], BF16)
    mark0 = k.arena_mark()
    xe = k.sb("l3a_xe", [128, T + 4], F32)
    xc = k.sb("l3a_xc", [128, T], F32)
    xsb = k.sb("l3a_xsb", [128, T], BF16)
    dtR = k.sb("l3a_dtR", [64, T], BF16)

    k.emit("gpsimd", lambda e: e.memset(xe[:, 0:4], 0.0), w=("l3a.xe",), tag="l3a xe0")
    MB = ("psB0", "psB1", "psC0", "psC1")
    for et in range(NE):
        if et == 0:
            inwS.load(0)
            wmS.load(0)
        for ch in range(4):
            bk = ("psA0", "psA1")[ch % 2]
            for ct in range(NCT):
                stat, wk_ = inwS.tile(et, ct)
                k.mm(PS[bk][:], stat, xap(ct, slice(ch * 512, (ch + 1) * 512)),
                     start=(ct == 0), stop=(ct == NCT - 1), r=(wk_,) + xkey(ct),
                     w=(bk,), tag=f"l3a xe{et}.{ch}")
            k.v("tensor_copy", xe[:, 4 + ch * 512: 4 + (ch + 1) * 512], PS[bk][:],
                r=(bk,), w=("l3a.xe",), tag=f"l3a xee{et}.{ch}")
        if et + 1 < NE:
            inwS.load(et + 1)
        k.v("tensor_single_scalar", xc[:], xe[:, 1:1 + T], convw[:, et * 4:et * 4 + 1],
            ALU.mult, r=("l3a.xe", "l3a.convw"), w=("l3a.xc",), tag=f"l3a cv0{et}")
        for kk2 in range(1, 4):
            k.v("scalar_tensor_tensor", xc[:], xe[:, 1 + kk2:1 + kk2 + T],
                convw[:, et * 4 + kk2:et * 4 + kk2 + 1], xc[:], ALU.mult, ALU.add,
                r=("l3a.xe", "l3a.convw", "l3a.xc"), w=("l3a.xc",), tag=f"l3a cv{kk2}{et}")
        k.act(xsb[:], xc[:], AF.Silu, bias=convb[:, et:et + 1],
              r=("l3a.xc", "l3a.convb"), w=("l3a.xs",), tag=f"l3a sl{et}")
        k.dmac(xs_d.ap()[et * 128:(et + 1) * 128, :], xsb[:], r=("l3a.xs",),
               w=(f"l3a.xsd{et}",), tag=f"l3a xsw{et}")
        statm, wkm = wmS.tile(et, 0)
        for ch in range(4):
            k.mm(PS[MB[ch]][:], statm, xsb[:, ch * 512:(ch + 1) * 512],
                 start=(et == 0), stop=(et == NE - 1),
                 r=(wkm, "l3a.xs"), w=(MB[ch],), tag=f"l3a M{et}.{ch}")
        if et + 1 < NE:
            wmS.load(et + 1)
    for ch in range(4):
        k.v("tensor_copy", BmT[:, ch * 512:(ch + 1) * 512], PS[MB[ch]][0:16, :],
            r=(MB[ch],), w=("l3a.Bm",), tag=f"l3a Bm{ch}")
        k.v("tensor_copy", CmT[:, ch * 512:(ch + 1) * 512], PS[MB[ch]][32:48, :],
            r=(MB[ch],), w=("l3a.Cm",), tag=f"l3a Cm{ch}")
        k.v("tensor_copy", dtR[:, ch * 512:(ch + 1) * 512], PS[MB[ch]][64:128, :],
            r=(MB[ch],), w=("l3a.dtR",), tag=f"l3a dR{ch}")

    for et in range(NE):
        if et == 0:
            skwS.load(0)
        for ch in range(4):
            bk = ("psA0", "psA1")[ch % 2]
            for ct in range(NCT):
                stat, wk_ = skwS.tile(et, ct)
                k.mm(PS[bk][:], stat, xap(ct, slice(ch * 512, (ch + 1) * 512)),
                     start=(ct == 0), stop=(ct == NCT - 1), r=(wk_,) + xkey(ct),
                     w=(bk,), tag=f"l3a sk{et}.{ch}")
            k.act(xsb[:, ch * 512:(ch + 1) * 512], PS[bk][:], AF.Silu, r=(bk,),
                  w=("l3a.xs",), tag=f"l3a ske{et}.{ch}")
        if et + 1 < NE:
            skwS.load(et + 1)
        k.dmac(ss_d.ap()[et * 128:(et + 1) * 128, :], xsb[:], r=("l3a.xs",),
               w=(f"l3a.ssd{et}",), tag=f"l3a ssw{et}")

    wd2S.nbuf = 2
    wd2S.slab = k.sb("ws_l3awd2b", [128, 2 * wd2S.n_c * wd2S.wmax], BF16)
    sp1 = k.sb("l3a_sp1", [128, T], F32)
    sp2 = k.sb("l3a_sp2", [128, T], F32)
    sp3 = k.sb("l3a_sp3", [128, T], F32)
    wd2S.load(0, b=0)
    for et in range(NE):
        if et + 1 < NE:
            wd2S.load(et + 1, b=(et + 1) % 2)
        stat, wkd = wd2S.tile(et, 0, b=et % 2)
        for ch in range(4):
            bk = ("psA0", "psA1")[ch % 2]
            k.mm(PS[bk][:], stat[0:64, :], dtR[0:64, ch * 512:(ch + 1) * 512],
                 True, True, r=(wkd, "l3a.dtR"), w=(bk,), tag=f"l3a dm{et}.{ch}")
            psl = slice(ch * 512, (ch + 1) * 512)
            k.v("tensor_scalar_add", sp1[:, psl], PS[bk][:], wd2b[:, et:et + 1],
                r=(bk, "l3a.wd2b"), w=("l3a.sp1",), tag=f"l3a pr{et}.{ch}")
        k.v("tensor_scalar_mul", sp2[:], sp1[:], -1.0,
            r=("l3a.sp1",), w=("l3a.sp2",), tag=f"l3a ab0{et}")
        k.v("tensor_max", sp2[:], sp2[:], sp1[:],
            r=("l3a.sp1", "l3a.sp2"), w=("l3a.sp2",), tag=f"l3a ab{et}")
        k.act(sp2[:], sp2[:], AF.Exp, scale=k.cc(-1.0), r=("l3a.sp2",), w=("l3a.sp2",),
              tag=f"l3a ex{et}")
        k.act(sp2[:], sp2[:], AF.Ln, bias=k.cc(1.0), r=("l3a.sp2",), w=("l3a.sp2",),
              tag=f"l3a ln{et}")
        k.v("tensor_scalar_max", sp3[:], sp1[:], 0.0, r=("l3a.sp1",), w=("l3a.sp3",),
            tag=f"l3a mx{et}")
        k.v("tensor_add", sp1[:], sp3[:], sp2[:], r=("l3a.sp3", "l3a.sp2", "l3a.sp1"),
            w=("l3a.sp1",), tag=f"l3a dt{et}")
        k.dmacast(dt_d.ap()[et * 128:(et + 1) * 128, :], sp1[:], r=("l3a.sp1",),
                  w=(f"l3a.dtd{et}",), tag=f"l3a dtw{et}")
        k.dmac(xsb[:], xs_d.ap()[et * 128:(et + 1) * 128, :], r=(f"l3a.xsd{et}", "l3a.xs"),
               w=("l3a.xs",), tag=f"l3a xsr{et}")
        k.v("tensor_mul", sp2[:], sp1[:], xsb[:], r=("l3a.sp1", "l3a.xs"),
            w=("l3a.sp2",), tag=f"l3a dx{et}")
        k.dmacast(dtx_d.ap()[et * 128:(et + 1) * 128, :], sp2[:], r=("l3a.sp2",),
                  w=(f"l3a.dxd{et}",), tag=f"l3a dxw{et}")
    k.arena_release(mark0)

    alog = k.sb("l3a_alog", [128, 16], F32)
    nA = k.sb("l3a_nA", [128, 16], F32)
    brow = k.sb("l3a_brow", [1, T], BF16)
    Bbc = k.sb("l3a_Bbc", [128, 4 * T], BF16)
    Cbc = k.sb("l3a_Cbc", [128, 4 * T], BF16)
    dtt = k.sb("l3a_dtt", [128, T], BF16)
    dxt = k.sb("l3a_dxt", [128, T], BF16)
    aa2 = k.sb("l3a_a", [128, 4 * T], BF16)
    bb2 = k.sb("l3a_b", [128, 4 * T], BF16)
    HH = k.sb("l3a_H", [128, T], F32)
    vv2 = k.sb("l3a_v", [128, T], F32)
    yy2 = k.sb("l3a_y", [128, T], F32)
    for g in range(4):
        for n in range(4):
            nn = g * 4 + n
            k.dmac(brow[:], BmT[nn:nn + 1, :], r=("l3a.Bm",), w=("l3a.brow",),
                   tag=f"l3a br{g}.{n}")
            for ch in range(4):
                bk = ("psA0", "psA1")[ch % 2]
                k.mm(PS[bk][:], k.ones_row[:], brow[0:1, ch * 512:(ch + 1) * 512],
                     True, True, r=("l3a.brow", "c.ones_row"), w=(bk,),
                     tag=f"l3a bb{g}.{n}.{ch}")
                k.v("tensor_copy", Bbc[:, n * T + ch * 512: n * T + (ch + 1) * 512],
                    PS[bk][:], r=(bk,), w=("l3a.Bbc",), tag=f"l3a bbe{g}.{n}.{ch}")
            k.dmac(brow[:], CmT[nn:nn + 1, :], r=("l3a.Cm",), w=("l3a.brow",),
                   tag=f"l3a cr{g}.{n}")
            for ch in range(4):
                bk = ("psA0", "psA1")[ch % 2]
                k.mm(PS[bk][:], k.ones_row[:], brow[0:1, ch * 512:(ch + 1) * 512],
                     True, True, r=("l3a.brow", "c.ones_row"), w=(bk,),
                     tag=f"l3a cc{g}.{n}.{ch}")
                k.v("tensor_copy", Cbc[:, n * T + ch * 512: n * T + (ch + 1) * 512],
                    PS[bk][:], r=(bk,), w=("l3a.Cbc",), tag=f"l3a cce{g}.{n}.{ch}")
        for et in range(NE):
            k.dmac(alog[:], W["l3a_alog"].ap()[et * 128:(et + 1) * 128, :],
                   w=("l3a.alog",), tag=f"l3a al{g}.{et}")
            k.act(nA[:], alog[:], AF.Exp, r=("l3a.alog",), w=("l3a.nA0",),
                  tag=f"l3a ae{g}.{et}")
            k.v("tensor_scalar_mul", nA[:], nA[:], -1.0, r=("l3a.nA0",),
                w=("l3a.nA",), tag=f"l3a an{g}.{et}")
            k.dmac(dtt[:], dt_d.ap()[et * 128:(et + 1) * 128, :], r=(f"l3a.dtd{et}",),
                   w=("l3a.dtt",), tag=f"l3a dtr{g}.{et}")
            k.dmac(dxt[:], dtx_d.ap()[et * 128:(et + 1) * 128, :], r=(f"l3a.dxd{et}",),
                   w=("l3a.dxt",), tag=f"l3a dxr{g}.{et}")
            a_bc = bass.AP(nA, g * 4, [[16, 128], [1, 4], [0, T]])
            dt_bc = bass.AP(dtt, 0, [[T, 128], [0, 4], [1, T]])
            k.v("tensor_tensor", aa2[:], a_bc, dt_bc, ALU.mult,
                r=("l3a.nA", "l3a.dtt"), w=("l3a.a",), tag=f"l3a z{g}.{et}")
            k.act(aa2[:], aa2[:], AF.Exp, r=("l3a.a",), w=("l3a.a",),
                  tag=f"l3a aexp{g}.{et}")
            dx_bc = bass.AP(dxt, 0, [[T, 128], [0, 4], [1, T]])
            k.v("tensor_tensor", bb2[:], dx_bc, Bbc[:], ALU.mult,
                r=("l3a.dxt", "l3a.Bbc"), w=("l3a.b",), tag=f"l3a bmu{g}.{et}")
            for n in range(4):
                k.v("tensor_tensor_scan", HH[:], aa2[:, n * T:(n + 1) * T],
                    bb2[:, n * T:(n + 1) * T], 0.0, ALU.mult, ALU.add,
                    r=("l3a.a", "l3a.b"), w=("l3a.H",), tag=f"l3a sc{g}.{et}.{n}")
                k.v("tensor_mul", vv2[:], HH[:], Cbc[:, n * T:(n + 1) * T],
                    r=("l3a.H", "l3a.Cbc"), w=("l3a.v",), tag=f"l3a vm{g}.{et}.{n}")
                if n == 0:
                    k.v("tensor_copy", yy2[:], vv2[:], r=("l3a.v",), w=("l3a.yy",),
                        tag=f"l3a y0{g}.{et}")
                else:
                    k.v("tensor_add", yy2[:], yy2[:], vv2[:], r=("l3a.yy", "l3a.v"),
                        w=("l3a.yy",), tag=f"l3a ya{g}.{et}.{n}")
            if g == 0:
                k.dmac(y_d.ap()[et * 128:(et + 1) * 128, :], yy2[:], r=("l3a.yy",),
                       w=(f"l3a.yd{et}",), tag=f"l3a yw{g}.{et}")
            else:
                k.emit("gpsimd",
                       (lambda e, et=et: e.dma_start(
                           y_d.ap()[et * 128:(et + 1) * 128, :], yy2[:],
                           accum_op=ALU.add)),
                       r=("l3a.yy", f"l3a.yd{et}"), w=(f"l3a.yd{et}",), dma=True,
                       tag=f"l3a ya{g}.{et}")
    k.arena_release(mark0)

    outwS = WStream(k, "l3aoutw", W["l3a_outw"], W["outw_meta"])
    oc = StreamCursor(outwS, [mt for tc in range(4) for mt in range(8)])
    ych = k.sb("l3a_ych", [128, 512], F32)
    xsch = k.sb("l3a_xsch", [128, 512], BF16)
    ssch = k.sb("l3a_ssch", [128, 512], BF16)
    Yp = k.sb("l3a_Yp", [128, NE * 512], BF16)
    xfp = k.sb("l3a_xfp", [128, 512], F32)
    xo = k.sb("l3a_xo", [128, 512], F32)
    for tc in range(4):
        tsl = slice(tc * 512, (tc + 1) * 512)
        for et in range(NE):
            k.dmac(ych[:], y_d.ap()[et * 128:(et + 1) * 128, tsl],
                   r=(f"l3a.yd{et}",), w=("l3a.ych",), tag=f"l3a yr{tc}.{et}")
            k.dmac(xsch[:], xs_d.ap()[et * 128:(et + 1) * 128, tsl],
                   r=(f"l3a.xsd{et}",), w=("l3a.xsch",), tag=f"l3a xr{tc}.{et}")
            k.dmac(ssch[:], ss_d.ap()[et * 128:(et + 1) * 128, tsl],
                   r=(f"l3a.ssd{et}",), w=("l3a.ssch",), tag=f"l3a sr{tc}.{et}")
            k.v("scalar_tensor_tensor", ych[:], xsch[:], wdcol[:, et:et + 1], ych[:],
                ALU.mult, ALU.add, r=("l3a.ych", "l3a.xsch", "l3a.wd"),
                w=("l3a.ych",), tag=f"l3a yp{tc}.{et}")
            k.v("tensor_mul", Yp[:, et * 512:(et + 1) * 512], ych[:], ssch[:],
                r=("l3a.ych", "l3a.ssch"), w=("l3a.Yp",), tag=f"l3a ym{tc}.{et}")
        for mtd in range(8):
            i = tc * 8 + mtd
            oc.use(i)
            for et in range(NE):
                st, wk_ = oc.tile(i, et)
                k.mm(PS["psB0"][:], st, Yp[:, et * 512:(et + 1) * 512],
                     start=(et == 0), stop=(et == NE - 1), r=(wk_, "l3a.Yp"),
                     w=("psB0",), tag=f"l3a om{tc}.{mtd}.{et}")
            k.dmac(xfp[:], x_in.ap()[mtd * 128:(mtd + 1) * 128, tsl],
                   w=("l3a.xfp",), tag=f"l3a rl{tc}.{mtd}")
            k.v("tensor_add", xo[:], xfp[:], PS["psB0"][:],
                r=("l3a.xfp", "psB0"), w=("l3a.xo",), tag=f"l3a ra{tc}.{mtd}")
            k.v("tensor_add", xo[:], xo[:],
                xl[:, mtd * T + tc * 512: mtd * T + (tc + 1) * 512],
                r=("l3a.xo", "l3l.xln"), w=("l3a.xo",), tag=f"l3a ra2{tc}.{mtd}")
            k.dmac(x_out.ap()[mtd * 128:(mtd + 1) * 128, tsl], xo[:],
                   r=("l3a.xo",), w=(f"xout.l3a.{mtd}.{tc}",), tag=f"l3a rs{tc}.{mtd}")
    return [f"xout.l3a.{mtd}.{tc}" for mtd in range(8) for tc in range(4)]


LAYERS = [
    ("l0a", prep_l0_attn, emit_l0_attn),
    ("l0f", prep_l0_ffn, emit_l0_ffn),
    ("l1a", prep_l1_attn, emit_l1_attn),
    ("l1f", prep_l1_ffn, emit_l1_ffn),
    ("l2a", prep_l2_attn, emit_l2_attn),
    ("l2f", prep_l2_ffn, emit_l2_ffn),
    ("l3a", prep_l3_attn, emit_l3_attn),
    ("l3f", prep_l3_ffn, emit_l3_ffn),
]


def kernel(**inputs):
    x = np.asarray(inputs["L0_attn_x"], np.float32)
    for nm, _, _ in LAYERS:
        x = run_layers([nm], x, inputs)
    return np.ascontiguousarray(x.astype(np.float32))
